# version 09
# baseline (speedup 1.0000x reference)
import numpy as np
import ml_dtypes
import concourse.bass as bass
import concourse.mybir as mybir
from concourse.bass_utils import run_bass_kernel_spmd
from contextlib import ExitStack

F32 = mybir.dt.float32
BF16 = mybir.dt.bfloat16
AF = mybir.ActivationFunctionType
ALU = mybir.AluOpType

B, LFULL, D = 4, 4096, 1024
import os as _os
L = int(_os.environ.get("MK_L", str(LFULL)))
E, NST, RNK, DCONV = 2048, 16, 64, 4
BETA, EPS = 0.6, 1e-5
TC = 512
NCH = L // TC
EBn = E // 128
DTn = D // 128
ADV = 64
NADV = TC // ADV
NCORES = 8
BF = ml_dtypes.bfloat16


class Prog:

    ENGS = ("pe", "dve", "act", "pool", "sp")

    QMAP = {"sp": "sp", "pool": "pool", "sc": "act"}

    def __init__(self):
        self.ops = {e: [] for e in self.ENGS}
        self.cnt = {e: 0 for e in self.ENGS}
        self.dcnt = {"sp": 0, "pool": 0, "sc": 0}
        self.wmap = {}
        self.rmap = {}

    def _deps(self, reads, writes):
        waits = []
        for rg in reads:
            t = self.wmap.get(rg)
            if t:
                waits.append(t)
        for rg in writes:
            t = self.wmap.get(rg)
            if t:
                waits.append(t)
            waits.extend(self.rmap.get(rg, ()))
        return waits

    def _commit(self, tok, reads, writes):
        for rg in reads:
            self.rmap.setdefault(rg, []).append(tok)
        for rg in writes:
            self.wmap[rg] = tok
            self.rmap[rg] = []

    def op(self, eng, fn, reads=(), writes=(), extra=()):
        waits = self._deps(reads, writes) + [w for w in extra if w]
        self.cnt[eng] += 1
        tok = (eng, self.cnt[eng])
        self.ops[eng].append(("op", tuple(waits), fn, None, None))
        self._commit(tok, reads, writes)
        return tok

    NSLOT = 8

    def dma(self, q, fn, reads=(), writes=(), extra=()):
        eng = self.QMAP[q]
        waits = self._deps(reads, writes) + [w for w in extra if w]
        idx = self.dcnt[q]
        self.dcnt[q] += 1
        slot = idx % self.NSLOT
        seq = idx // self.NSLOT + 1
        tok = (f"{q}D{slot}", seq)
        self.ops[eng].append(("dma", tuple(waits), fn, slot, seq, q))
        self._commit(tok, reads, writes)
        return tok

    def barrier(self, eng, reads):
        waits = self._deps(reads, ())
        self.ops[eng].append(("wait", tuple(waits), None, None, None))


def build_nc(Lb=None, dbg=False, phase=9, sub=9):
    global L, NCH
    if Lb is not None:
        L = Lb
        NCH = L // TC
    nc = bass.Bass("TRN2", target_bir_lowering=False)

    xT_d = nc.dram_tensor("xT", [D, L], F32, kind="ExternalInput").ap()
    lw = {}
    for i in range(2):
        p = f"L{i}_"
        for nm, shp in (("inw_xbT", [D, E]), ("inw_zT", [D, E]), ("convw", [128, EBn * DCONV]),
                        ("convb", [128, EBn]), ("xpdT", [128, EBn * RNK]), ("xpBT", [128, EBn * NST]),
                        ("xpCT", [128, EBn * NST]), ("dtwT", [RNK, E]), ("dtb", [128, EBn]),
                        ("alog", [128, EBn * NST]), ("dp", [128, EBn]), ("owT", [E, D]),
                        ("normw", [128, DTn])):
            lw[p + nm] = nc.dram_tensor(p + nm, shp, F32, kind="ExternalInput").ap()
    wr_d = nc.dram_tensor("Wr", [D, D], F32, kind="ExternalInput").ap()
    wi_d = nc.dram_tensor("Wi", [D, D], F32, kind="ExternalInput").ap()
    fbias_d = nc.dram_tensor("fbias", [128, DTn], F32, kind="ExternalInput").ap()
    ident16_d = nc.dram_tensor("ident16", [128, 128], BF16, kind="ExternalInput").ap()
    betap_d = nc.dram_tensor("betap16", [128, ADV], F32, kind="ExternalInput").ap()
    onesrow_d = nc.dram_tensor("onesrow16", [1, 128], BF16, kind="ExternalInput").ap()
    onescol_d = nc.dram_tensor("onescol16", [128, 1], BF16, kind="ExternalInput").ap()
    sel16_d = nc.dram_tensor("sel16", [NST, NST * 128], BF16, kind="ExternalInput").ap()
    cfh_d = nc.dram_tensor("Cf_hi", [DTn * 2, 128, 512], BF16, kind="ExternalInput").ap()
    cfl_d = nc.dram_tensor("Cf_lo", [DTn * 2, 128, 512], BF16, kind="ExternalInput").ap()
    sfh_d = nc.dram_tensor("Sf_hi", [DTn * 2, 128, 512], BF16, kind="ExternalInput").ap()
    sfl_d = nc.dram_tensor("Sf_lo", [DTn * 2, 128, 512], BF16, kind="ExternalInput").ap()
    snh_d = nc.dram_tensor("Sfn_hi", [DTn * 2, 128, 512], BF16, kind="ExternalInput").ap()
    snl_d = nc.dram_tensor("Sfn_lo", [DTn * 2, 128, 512], BF16, kind="ExternalInput").ap()
    crow_d = nc.dram_tensor("constrow", [128, 8], F32, kind="ExternalInput").ap()
    yT_d = nc.dram_tensor("yT", [D, L], F32, kind="ExternalOutput").ap()
    mdbg_d = nc.dram_tensor("m_dbg", [D, D], BF16, kind="ExternalOutput").ap() if dbg else None

    scr = {}
    for i in range(2):
        p = f"L{i}_"
        scr[p + "wxb"] = nc.dram_tensor(p + "s_wxb", [EBn, 128, DTn * 128], BF16)
        scr[p + "wz"] = nc.dram_tensor(p + "s_wz", [EBn, 128, DTn * 128], BF16)
        scr[p + "wo"] = nc.dram_tensor(p + "s_wo", [DTn, 128, EBn * 128], BF16)
    scr["m"] = nc.dram_tensor("s_m", [DTn, 128, DTn * 128], BF16)
    scr["ml"] = nc.dram_tensor("s_ml", [DTn, 128, DTn * 128], BF16)
    for wnm in ("wrh", "wrl", "wih", "wil"):
        scr[wnm] = nc.dram_tensor("s_" + wnm, [D, D], BF16)

    P = Prog()

    with ExitStack() as ctx:
        en = ctx.enter_context

        ident16 = en(nc.sbuf_tensor("ident16_s", [128, 128], BF16))
        betap16 = en(nc.sbuf_tensor("betap16_s", [128, ADV], F32))
        onesrow16 = en(nc.sbuf_tensor("onesrow16_s", [1, 128], BF16))
        onescol16 = en(nc.sbuf_tensor("onescol16_s", [128, 1], BF16))
        sel16 = en(nc.sbuf_tensor("sel16_s", [NST, NST * 128], BF16))
        xpd16 = [en(nc.sbuf_tensor(f"xpd16_{i}", [128, EBn * RNK], BF16)) for i in range(2)]
        xpB16 = [en(nc.sbuf_tensor(f"xpB16_{i}", [128, EBn * NST], BF16)) for i in range(2)]
        xpC16 = [en(nc.sbuf_tensor(f"xpC16_{i}", [128, EBn * NST], BF16)) for i in range(2)]
        dtw16p = en(nc.sbuf_tensor("dtw16p", [128, E], BF16))
        convb = [en(nc.sbuf_tensor(f"convb_{i}", [128, EBn], F32)) for i in range(2)]
        cwsb = [en(nc.sbuf_tensor(f"cwsb_{i}", [128, EBn * DCONV], F32)) for i in range(2)]
        dtb = [en(nc.sbuf_tensor(f"dtb_{i}", [128, EBn], F32)) for i in range(2)]
        dpv = [en(nc.sbuf_tensor(f"dp_{i}", [128, EBn], F32)) for i in range(2)]
        normw = [en(nc.sbuf_tensor(f"normw_{i}", [128, DTn], F32)) for i in range(2)]
        wneg = [en(nc.sbuf_tensor(f"wneg_{i}", [128, EBn * NST], F32)) for i in range(2)]
        fbias = en(nc.sbuf_tensor("fbias_s", [128, DTn], F32))

        x0 = en(nc.sbuf_tensor("x0", [128, DTn * TC], F32))
        rres = en(nc.sbuf_tensor("rres", [128, DTn * TC], F32))
        big16 = en(nc.sbuf_tensor("big16", [128, 2 * DTn * TC], BF16))
        xn16 = big16[:, 0:DTn * TC]
        sq16 = big16[:, DTn * TC:2 * DTn * TC]
        xzr = en(nc.sbuf_tensor("xzr", [128, 2 * (TC + 3)], F32))
        xztail = [en(nc.sbuf_tensor(f"xztail_{i}", [128, EBn * 3], F32)) for i in range(2)]
        xb16 = en(nc.sbuf_tensor("xb16", [128, EBn * TC], BF16))
        u16 = en(nc.sbuf_tensor("u16", [128, 2 * (TC + ADV)], BF16))
        utail = [en(nc.sbuf_tensor(f"utail_{i}", [128, EBn * ADV], BF16)) for i in range(2)]
        b16 = en(nc.sbuf_tensor("b16", [NST, TC + ADV], BF16))
        btail = [en(nc.sbuf_tensor(f"btail_{i}", [NST, ADV], BF16)) for i in range(2)]
        c16 = en(nc.sbuf_tensor("c16", [NST, TC], BF16))
        dbcd16 = en(nc.sbuf_tensor("dbcd16", [128, TC], BF16))
        btw16 = en(nc.sbuf_tensor("btw16", [128, NADV * NST], BF16))
        utw16 = en(nc.sbuf_tensor("utw16", [128, 2 * NADV * 128], BF16))
        wbeta = en(nc.sbuf_tensor("wbeta", [128, NADV * NST * ADV], BF16))
        delta = en(nc.sbuf_tensor("delta", [128, 2 * TC], F32))
        spl = en(nc.sbuf_tensor("spl", [128, TC], F32))
        a_f = en(nc.sbuf_tensor("a_f", [128, 4 * TC], F32))
        h16 = en(nc.sbuf_tensor("h16", [128, NST * TC], BF16))
        crep = en(nc.sbuf_tensor("crep", [128, NST * TC], BF16))
        t1f = en(nc.sbuf_tensor("t1f", [128, TC], F32))
        sz16 = en(nc.sbuf_tensor("sz16", [128, 3 * TC], BF16))
        y3a = en(nc.sbuf_tensor("y3a", [128, EBn * TC], BF16))
        carry = [en(nc.sbuf_tensor(f"carry_{i}", [128, EBn * NST], BF16)) for i in range(2)]
        wA = en(nc.sbuf_tensor("wA", [128, 2 * DTn * 128], BF16))
        wB = en(nc.sbuf_tensor("wB", [128, 2 * EBn * 128], BF16))
        crow = en(nc.sbuf_tensor("crow", [128, 8], F32))
        mr = en(nc.sbuf_tensor("mr", [1, TC], F32))
        rstdr = en(nc.sbuf_tensor("rstdr", [1, TC], F32))
        tmpr = en(nc.sbuf_tensor("tmpr", [1, TC], F32))
        mm16 = en(nc.sbuf_tensor("mm16", [1, 2 * TC], BF16))
        rs16 = en(nc.sbuf_tensor("rs16", [1, 2 * TC], BF16))

        ps_a = en(nc.psum_tensor("ps_a", [128, 512], F32))
        ps_b = en(nc.psum_tensor("ps_b", [128, 512], F32))
        ps_v = en(nc.psum_tensor("ps_v", [128, 2048], F32))
        ps_t = en(nc.psum_tensor("ps_t", [128, 1024], BF16))
        ps_s = en(nc.psum_tensor("ps_s", [128, 512], F32))

        semnames = ["pe", "dve", "act", "pool", "sp"]
        for q in ("sp", "pool", "sc"):
            for s in range(Prog.NSLOT):
                semnames.append(f"{q}D{s}")
        sems = {n: en(nc.semaphore("sem_" + n)) for n in semnames}

        block = en(nc.Block())

        _CONSTS = [EPS, 1.0, -1.0, -0.5, 1.0 / D, 0.0, 2.0, 0.5]
        for _ci, _cv in enumerate(_CONSTS):
            nc.const_aps.register(F32, _cv, crow[:, _ci:_ci + 1])

        def mm(out, lhsT, rhs, reads, writes, start=True, stop=True, extra=()):
            return P.op("pe", lambda t, o=out, l=lhsT, r=rhs, s0=start, s1=stop:
                        t.matmul(o, l, r, start=s0, stop=s1), reads, writes, extra)

        def transp(out, in_, ident_ap, reads, writes):
            return P.op("pe", lambda t, o=out, i=in_, d=ident_ap:
                        t.transpose(o, i, d), reads, writes)

        def act(out, in_, func, reads, writes, bias=0.0, scale=1.0):
            rd = list(reads)
            if isinstance(bias, float) or isinstance(scale, float):
                rd.append("crow")
            return P.op("act", lambda s, o=out, i=in_, f=func, b=bias, sc=scale:
                        s.activation(o, i, f, bias=b, scale=sc), rd, writes)

        def tt(eng, out, i0, i1, op, reads, writes):
            return P.op(eng, lambda v, o=out, a0=i0, a1=i1, p=op:
                        v.tensor_tensor(o, a0, a1, p), reads, writes)

        def tsc(eng, out, i0, sc1, sc2, op0, op1, reads, writes):
            return P.op(eng, lambda v, o=out, a0=i0, x=sc1, y=sc2, p0=op0, p1=op1:
                        v.tensor_scalar(o, a0, x, y, p0, p1), reads, writes)

        def stt(eng, out, i0, sc, i1, op0, op1, reads, writes):
            return P.op(eng, lambda v, o=out, a0=i0, x=sc, a1=i1, p0=op0, p1=op1:
                        v.scalar_tensor_tensor(o, a0, x, a1, p0, p1), reads, writes)

        def cpy(eng, out, in_, reads, writes):
            if eng == "act":
                return P.op(eng, lambda v, o=out, i=in_: v.copy(o, i), reads, writes)
            return P.op(eng, lambda v, o=out, i=in_: v.tensor_copy(o, i), reads, writes)

        def scan(out, d0, d1, init, reads, writes):
            return P.op("dve", lambda v, o=out, a0=d0, a1=d1, ii=init:
                        v.tensor_tensor_scan(o, a0, a1, ii, ALU.mult, ALU.add),
                        reads, writes)

        def mset(eng, ap_, val, reads, writes):
            return P.op(eng, lambda v, a0=ap_, x=val: v.memset(a0, x), reads, writes)

        def dsp(out, in_, reads, writes):
            return P.dma("sp", lambda s, o=out, i=in_: s.dma_start(out=o, in_=i),
                         reads, writes)

        def dgp(out, in_, reads, writes):
            return P.dma("pool", lambda g, o=out, i=in_: g.dma_start(out=o, in_=i),
                         reads, writes)

        def dsc(out, in_, reads, writes):
            return P.dma("sc", lambda s, o=out, i=in_: s.dma_start(out=o, in_=i),
                         reads, writes)

        _dalt = [0]

        def dwt(out, in_, reads, writes):
            _dalt[0] ^= 1
            return (dsp if _dalt[0] else dsc)(out, in_, reads, writes)

        dsp(crow[:], crow_d, [], ["crow"])
        dsp(ident16[:], ident16_d, [], ["ident"])
        dsp(betap16[:], betap_d, [], ["betap"])
        dsp(onesrow16[:], onesrow_d, [], ["onesrow"])
        dsp(onescol16[:], onescol_d, [], ["onescol"])
        dsp(sel16[:], sel16_d, [], ["sel"])
        dsp(fbias[:], fbias_d, [], ["fbias"])
        for i in range(2):
            p = f"L{i}_"
            dgp(xpd16[i][:], lw[p + "xpdT"], [], [f"xpd{i}"])
            dgp(xpB16[i][:], lw[p + "xpBT"], [], [f"xpB{i}"])
            dgp(xpC16[i][:], lw[p + "xpCT"], [], [f"xpC{i}"])
            dgp(dtw16p[RNK * i:RNK * (i + 1), :], lw[p + "dtwT"], [], [f"dtw{i}"])
            dsp(convb[i][:], lw[p + "convb"], [], [f"convb{i}"])
            dsp(cwsb[i][:], lw[p + "convw"], [], [f"cw{i}"])
            dsp(dtb[i][:], lw[p + "dtb"], [], [f"dtb{i}"])
            dsp(dpv[i][:], lw[p + "dp"], [], [f"dp{i}"])
            dsp(normw[i][:], lw[p + "normw"], [], [f"normw{i}"])
            dsp(wneg[i][:], lw[p + "alog"], [], [f"wneg{i}"])
            act(wneg[i][:], wneg[i][:], AF.Exp, [f"wneg{i}"], [f"wneg{i}"])
            tsc("dve", wneg[i][:], wneg[i][:], -1.0, None, ALU.mult, ALU.bypass,
                [f"wneg{i}"], [f"wneg{i}"])
            for _eb in range(EBn):
                dgp(scr[p + "wxb"].ap().rearrange("eb p (kc c) -> eb p kc c", c=128)[_eb],
                    lw[p + "inw_xbT"].rearrange("(kc p) (eb c) -> eb p kc c", p=128, c=128)[_eb],
                    [], [f"swxb{i}"])
                dgp(scr[p + "wz"].ap().rearrange("eb p (kc c) -> eb p kc c", c=128)[_eb],
                    lw[p + "inw_zT"].rearrange("(kc p) (eb c) -> eb p kc c", p=128, c=128)[_eb],
                    [], [f"swz{i}"])
            for _mt in range(DTn):
                dgp(scr[p + "wo"].ap().rearrange("mt p (kc c) -> mt p kc c", c=128)[_mt],
                    lw[p + "owT"].rearrange("(kc p) (mt c) -> mt p kc c", p=128, c=128)[_mt],
                    [], [f"swo{i}"])
            mset("dve", carry[i][:], 0.0, [], [f"carry{i}"])
            mset("dve", utail[i][:], 0.0, [], [f"utail{i}"])
            mset("dve", btail[i][:], 0.0, [], [f"btail{i}"])
            mset("dve", xztail[i][:], 0.0, [], [f"xztail{i}"])

        wstg_h = big16[:, 0:D]
        wstg_l = big16[:, D:2 * D]
        wstg_f = xb16[:, 0:2 * D].bitcast(F32)
        for (wd, hnm, lnm) in ((wr_d, "wrh", "wrl"), (wi_d, "wih", "wil")):
            for kc in range(DTn):
                dsp(wstg_f, wd[kc * 128:(kc + 1) * 128, :], ["wsl"], ["wsf"])
                cpy("act", wstg_h, wstg_f, ["wsf"], ["wsh"])
                tt("dve", wstg_l, wstg_f, wstg_h, ALU.subtract, ["wsf", "wsh"], ["wsl"])
                dsp(scr[hnm][kc * 128:(kc + 1) * 128, :], wstg_h, ["wsh"], ["scrw"])
                dsp(scr[lnm][kc * 128:(kc + 1) * 128, :], wstg_l, ["wsl"], ["scrw"])

        trh, trl = xb16[:], big16[:]
        tih, til = crep[:], wbeta[:]
        cstr0 = delta[:].bitcast(BF16)
        cstr1 = a_f[:, 0:2 * TC].bitcast(BF16)

        def load_cs(slot, kc, nh, s_hi_d, s_lo_d):
            cs = cstr0 if slot == 0 else cstr1
            rg = f"cs{slot}"
            t = kc * 2 + nh
            dsp(cs[:, 0:512], cfh_d[t], [], [rg + "a"])
            dsc(cs[:, 512:1024], cfl_d[t], [], [rg + "b"])
            dsp(cs[:, 1024:1536], s_hi_d[t], [], [rg + "c"])
            dsc(cs[:, 1536:2048], s_lo_d[t], [], [rg + "d"])
            return cs

        wsl_rh = y3a[:, 0:D]
        wsl_rl = y3a[:, D:2 * D]
        wsl_ih = y3a[:, 2 * D:3 * D]
        wsl_il = y3a[:, 3 * D:4 * D]
        for ot in range(DTn):
            for (nmh, sl_ap) in (("wrh", wsl_rh), ("wrl", wsl_rl), ("wih", wsl_ih), ("wil", wsl_il)):
                dwt(sl_ap.rearrange("p (kc c) -> p kc c", kc=DTn),
                    scr[nmh][:, ot * 128:(ot + 1) * 128].rearrange("(kc p) c -> p kc c", p=128),
                    ["scrw", "y3aW"], ["y3aW"])
            for pass_ti in (0, 1):
                dsth = trh if pass_ti == 0 else tih
                dstl = trl if pass_ti == 0 else til
                hi_rg = "trh" if pass_ti == 0 else "tih"
                lo_rg = "trl" if pass_ti == 0 else "til"
                s_hi, s_lo = (sfh_d, sfl_d) if pass_ti == 0 else (snh_d, snl_d)
                wa_h, wa_l = (wsl_rh, wsl_rl) if pass_ti == 0 else (wsl_ih, wsl_il)
                wb_h, wb_l = (wsl_ih, wsl_il) if pass_ti == 0 else (wsl_rh, wsl_rl)
                for nh in range(2):
                    gslot = (ot * 4 + pass_ti * 2 + nh) % 2
                    psrg = "ps_a" if gslot == 0 else "ps_b"
                    pps = ps_a if gslot == 0 else ps_b
                    first = True
                    for kc in range(DTn):
                        slot = kc % 2
                        cs = load_cs(slot, kc, nh, s_hi, s_lo)
                        rg = f"cs{slot}"
                        ksl = slice(kc * 128, (kc + 1) * 128)
                        combos = ((wa_h[:, ksl], cs[:, 0:512], rg + "a"),
                                  (wa_h[:, ksl], cs[:, 512:1024], rg + "b"),
                                  (wa_l[:, ksl], cs[:, 0:512], rg + "a"),
                                  (wb_h[:, ksl], cs[:, 1024:1536], rg + "c"),
                                  (wb_h[:, ksl], cs[:, 1536:2048], rg + "d"),
                                  (wb_l[:, ksl], cs[:, 1024:1536], rg + "c"))
                        for ci, (wop, cop, crg) in enumerate(combos):
                            mm(pps[:, 0:512], wop, cop, ["y3aW", crg], [psrg],
                               start=first, stop=(kc == DTn - 1 and ci == 5))
                            first = False
                    dst_sl = slice(ot * D + nh * 512, ot * D + (nh + 1) * 512)
                    act(dsth[:, dst_sl], pps[:, 0:512], AF.Copy, [psrg], [hi_rg])
                    tt("dve", dstl[:, dst_sl], pps[:, 0:512], dsth[:, dst_sl],
                       ALU.subtract, [psrg, hi_rg], [lo_rg])

        for mt in range(DTn):
            for nh in range(2):
                gslot = (mt * 2 + nh) % 2
                psrg = "ps_a" if gslot == 0 else "ps_b"
                pps = ps_a if gslot == 0 else ps_b
                first = True
                for kc in range(DTn):
                    slot = kc % 2
                    cs = load_cs(slot, kc, nh, snh_d, snl_d)
                    rg = f"cs{slot}"
                    sl_k = slice(kc * D + mt * 128, kc * D + (mt + 1) * 128)
                    mm(pps[:, 0:512], trh[:, sl_k], cs[:, 0:512], ["trh", rg + "a"], [psrg],
                       start=first, stop=False)
                    first = False
                    mm(pps[:, 0:512], trh[:, sl_k], cs[:, 512:1024], ["trh", rg + "b"], [psrg],
                       start=False, stop=False)
                    mm(pps[:, 0:512], trl[:, sl_k], cs[:, 0:512], ["trl", rg + "a"], [psrg],
                       start=False, stop=False)
                    mm(pps[:, 0:512], tih[:, sl_k], cs[:, 1024:1536], ["tih", rg + "c"], [psrg],
                       start=False, stop=False)
                    mm(pps[:, 0:512], tih[:, sl_k], cs[:, 1536:2048], ["tih", rg + "d"], [psrg],
                       start=False, stop=False)
                    mm(pps[:, 0:512], til[:, sl_k], cs[:, 1024:1536], ["til", rg + "c"], [psrg],
                       start=False, stop=(kc == DTn - 1))
                act(y3a[:, 0:512], pps[:, 0:512], AF.Copy, [psrg], ["mstg"],
                    scale=1.0 / D)
                stt("dve", y3a[:, 512:1024], pps[:, 0:512], 1.0 / D, y3a[:, 0:512],
                    ALU.mult, ALU.subtract, [psrg, "mstg"], ["mstg"])
                mdst = scr["m"].ap().rearrange("mo p (kc c) -> mo p kc c", c=128)[
                    nh * 4:(nh + 1) * 4, :, mt, :].rearrange("m p f -> p m f")
                mldst = scr["ml"].ap().rearrange("mo p (kc c) -> mo p kc c", c=128)[
                    nh * 4:(nh + 1) * 4, :, mt, :].rearrange("m p f -> p m f")
                dsp(mdst, y3a[:, 0:512].rearrange("p (m f) -> p m f", m=4),
                    ["mstg"], ["scrm"])
                dsp(mldst, y3a[:, 512:1024].rearrange("p (m f) -> p m f", m=4),
                    ["mstg"], ["scrm"])

        def ln_stats(src_ap, src_rg, need_mean):
            if need_mean:
                cpy("dve", sq16, src_ap, [src_rg], ["sq16"])
                for mt in range(DTn):
                    mm(ps_s[0:1, 0:512], onescol16[:], sq16[:, mt * TC:(mt + 1) * TC],
                       ["sq16", "onescol"], ["ps_s"], start=(mt == 0), stop=(mt == DTn - 1))
            act(sq16, src_ap, AF.Square, [src_rg], ["sq16", "dpx"])
            for mt in range(DTn):
                mm(ps_b[0:1, 0:512], onescol16[:], sq16[:, mt * TC:(mt + 1) * TC],
                   ["sq16", "onescol"], ["ps_b"], start=(mt == 0), stop=(mt == DTn - 1))

        def ln_rows(rms_only):
            if rms_only:
                tsc("dve", tmpr[:], ps_b[0:1, 0:512], 1.0 / D, None, ALU.mult, ALU.bypass,
                    ["ps_b"], ["tmpr"])
            else:
                tsc("dve", mr[:], ps_s[0:1, 0:512], 1.0 / D, None, ALU.mult, ALU.bypass,
                    ["ps_s"], ["mr"])
                tsc("dve", tmpr[:], ps_b[0:1, 0:512], 1.0 / D, None, ALU.mult, ALU.bypass,
                    ["ps_b"], ["tmpr"])
                tt("dve", rstdr[:], mr[:], mr[:], ALU.mult, ["mr"], ["rstdr"])
                tt("dve", tmpr[:], tmpr[:], rstdr[:], ALU.subtract, ["tmpr", "rstdr"], ["tmpr"])
            act(tmpr[:], tmpr[:], AF.Ln, ["tmpr"], ["tmpr"], bias=EPS)
            act(rstdr[:], tmpr[:], AF.Exp, ["tmpr"], ["rstdr"], scale=-0.5)

        def hilo_row(dst16, dst_rg, src_row, src_rg):
            cpy("dve", dst16[:, 0:TC], src_row, [src_rg], [dst_rg])
            tt("dve", dst16[:, TC:2 * TC], src_row, dst16[:, 0:TC], ALU.subtract,
               [src_rg, dst_rg], [dst_rg])

        def bcast2(dst_ps, ps_rg, row16, row_rg):
            mm(dst_ps, onesrow16[0:1, :], row16[:, 0:TC], [row_rg, "onesrow"], [ps_rg],
               start=True, stop=False)
            mm(dst_ps, onesrow16[0:1, :], row16[:, TC:2 * TC], [row_rg, "onesrow"], [ps_rg],
               start=False, stop=True)

        def ln_apply_plain(src_ap, src_rg, dst_f32_ap, dst_rg, add_to=None, add_rg=None,
                           dst16_ap=None, dst16_rg=None):
            tt("dve", tmpr[:], mr[:], rstdr[:], ALU.mult, ["mr", "rstdr"], ["tmpr"])
            hilo_row(rs16, "rs16", rstdr[:], "rstdr")
            hilo_row(mm16, "mm16", tmpr[:], "tmpr")
            bcast2(ps_s[:, 0:512], "ps_s", rs16, "rs16")
            bcast2(ps_b[:, 0:512], "ps_b", mm16, "mm16")
            for mt in range(DTn):
                sl = slice(mt * TC, (mt + 1) * TC)
                if add_to is not None:
                    tt("dve", t1f[:], src_ap[:, sl], ps_s[:, 0:512], ALU.mult,
                       [src_rg, "ps_s"], ["t1f"])
                    tt("dve", t1f[:], t1f[:], ps_b[:, 0:512], ALU.subtract,
                       ["t1f", "ps_b"], ["t1f"])
                    tt("dve", add_to[:, sl], add_to[:, sl], t1f[:], ALU.add,
                       [add_rg, "t1f"], [add_rg])
                elif dst16_ap is not None:
                    tt("dve", t1f[:], src_ap[:, sl], ps_s[:, 0:512], ALU.mult,
                       [src_rg, "ps_s"], ["t1f"])
                    tt("dve", dst16_ap[:, sl], t1f[:], ps_b[:, 0:512], ALU.subtract,
                       ["t1f", "ps_b"], [dst16_rg])
                else:
                    tt("dve", dst_f32_ap[:, sl], src_ap[:, sl], ps_s[:, 0:512], ALU.mult,
                       [src_rg, "ps_s"], [dst_rg])
                    tt("dve", dst_f32_ap[:, sl], dst_f32_ap[:, sl], ps_b[:, 0:512],
                       ALU.subtract, [dst_rg, "ps_b"], [dst_rg])

        def emit_mamba(k, i):
            if phase < 2:
                return
            ln_stats(rres[:], "rres", False)
            ln_rows(True)
            hilo_row(rs16, "rs16", rstdr[:], "rstdr")
            bcast2(ps_s[:, 0:512], "ps_s", rs16, "rs16")
            for mt in range(DTn):
                sl = slice(mt * TC, (mt + 1) * TC)
                tt("dve", t1f[:], rres[:, sl], ps_s[:, 0:512], ALU.mult,
                   ["rres", "ps_s"], ["t1f"])
                act(xn16[:, sl], t1f[:], AF.Copy, ["t1f", f"normw{i}"], ["xn16"],
                    scale=normw[i][:, mt:mt + 1])

            for eb in range(EBn):
                slp = eb % 2
                wsl = wA[:, slp * DTn * 128:(slp + 1) * DTn * 128]
                wrg = f"wA{slp}"
                dsp(wsl, scr[f"L{i}_wxb"][eb], [f"swxb{i}"], [wrg])
                for kc in range(DTn):
                    mm(ps_a[:, 0:512], wsl[:, kc * 128:(kc + 1) * 128],
                       xn16[:, kc * TC:(kc + 1) * TC], [wrg, "xn16"], ["ps_a"],
                       start=(kc == 0), stop=(kc == DTn - 1))
                xzs = xzr[:, slp * (TC + 3):(slp + 1) * (TC + 3)]
                xrg = f"xzr{slp}"
                act(xzs[:, 3:TC + 3], ps_a[:, 0:512], AF.Copy, ["ps_a"], [xrg])
                cpy("pool", xzs[:, 0:3], xztail[i][:, eb * 3:eb * 3 + 3],
                    [f"xztail{i}"], [xrg])
                cpy("pool", xztail[i][:, eb * 3:eb * 3 + 3], xzs[:, TC:TC + 3],
                    [xrg], [f"xztail{i}"])
                dacc = delta[:, slp * TC:(slp + 1) * TC]
                drg = f"delta{slp}"
                tsc("dve", dacc, xzs[:, 0:TC], cwsb[i][:, eb * DCONV:eb * DCONV + 1],
                    None, ALU.mult, ALU.bypass, [xrg, f"cw{i}"], [drg])
                for kk in range(1, DCONV):
                    stt("dve", dacc, xzs[:, kk:kk + TC],
                        cwsb[i][:, eb * DCONV + kk:eb * DCONV + kk + 1], dacc,
                        ALU.mult, ALU.add, [xrg, f"cw{i}", drg], [drg])
                act(xb16[:, eb * TC:(eb + 1) * TC], dacc, AF.Silu,
                    [drg, f"convb{i}"], ["xb16"], bias=convb[i][:, eb:eb + 1])

            if phase < 3:
                return
            for kc in range(EBn):
                mm(ps_a[0:RNK, 0:512], xpd16[i][:, kc * RNK:(kc + 1) * RNK],
                   xb16[:, kc * TC:(kc + 1) * TC], [f"xpd{i}", "xb16"], ["ps_a"],
                   start=(kc == 0), stop=(kc == EBn - 1))
                mm(ps_b[0:NST, 0:512], xpB16[i][:, kc * NST:(kc + 1) * NST],
                   xb16[:, kc * TC:(kc + 1) * TC], [f"xpB{i}", "xb16"], ["ps_b"],
                   start=(kc == 0), stop=(kc == EBn - 1))
                mm(ps_s[0:NST, 0:512], xpC16[i][:, kc * NST:(kc + 1) * NST],
                   xb16[:, kc * TC:(kc + 1) * TC], [f"xpC{i}", "xb16"], ["ps_s"],
                   start=(kc == 0), stop=(kc == EBn - 1))
            act(dbcd16[RNK * i:RNK * (i + 1), :], ps_a[0:RNK, 0:512], AF.Copy, ["ps_a"], ["dbcd"])
            cpy("pool", b16[:, 0:ADV], btail[i][:], [f"btail{i}"], ["b16"])
            act(b16[:, ADV:ADV + TC], ps_b[0:NST, 0:512], AF.Copy, ["ps_b"], ["b16"])
            cpy("pool", btail[i][:], b16[:, TC:TC + ADV], ["b16"], [f"btail{i}"])
            act(c16[:], ps_s[0:NST, 0:512], AF.Copy, ["ps_s"], ["c16"])

            for a in range(NADV):
                transp(ps_t[:, a * NST:(a + 1) * NST], b16[0:NST, a * ADV: a * ADV + 128],
                       ident16[0:NST, 0:NST], ["b16", "ident"], ["ps_t0"])
            cpy("dve", btw16[:], ps_t[:, 0:NADV * NST], ["ps_t0"], ["btw"])
            for a in range(NADV):
                o = wbeta[:, a * NST * ADV:(a + 1) * NST * ADV].rearrange(
                    "p (n j) -> p n j", n=NST)
                i0 = betap16[:].unsqueeze(1).broadcast_to([128, NST, ADV])
                i1 = btw16[:, a * NST:(a + 1) * NST].unsqueeze(2).broadcast_to([128, NST, ADV])
                tt("dve", o, i0, i1, ALU.mult, ["btw", "betap"], [f"wbeta{a}"])

            for n in range(NST):
                slc = n % 2
                prg = f"ps_v{slc}"
                mm(ps_v[:, slc * 1024: slc * 1024 + 512],
                   sel16[:, n * 128:(n + 1) * 128], c16[:], ["sel", "c16"], [prg])
                cpy("dve", crep[:, n * TC:(n + 1) * TC],
                    ps_v[:, slc * 1024: slc * 1024 + 512], [prg], ["crep"])

            if phase < 4:
                return
            hf = h16[:].bitcast(F32)
            HLO = [f"h{n2}" for n2 in range(8)]
            HHI = [f"h{n2}" for n2 in range(8, NST)]

            def dp_y3(eb):
                dpx = big16[:, 6144:6656]
                tsc("dve", dpx, xb16[:, eb * TC:(eb + 1) * TC], dpv[i][:, eb:eb + 1],
                    None, ALU.mult, ALU.bypass, ["xb16", f"dp{i}"], ["dpx"])
                tt("pool", t1f[:], t1f[:], dpx, ALU.add, ["t1f", "dpx"], ["t1f"])

            def y3_emit(eb):
                szs = sz16[:, (eb % 3) * TC:(eb % 3 + 1) * TC]
                tt("pool", y3a[:, eb * TC:(eb + 1) * TC], t1f[:], szs, ALU.mult,
                   ["t1f", f"sz{eb % 3}"], ["y3a"])

            def z_jit(eb):
                slp = eb % 2
                wsl2 = wB[:, slp * EBn * 128: slp * EBn * 128 + DTn * 128]
                zrg = f"wB{slp}"
                dsp(wsl2, scr[f"L{i}_wz"][eb], [f"swz{i}"], [zrg])
                for kc in range(DTn):
                    mm(ps_b[:, 0:512], wsl2[:, kc * 128:(kc + 1) * 128],
                       xn16[:, kc * TC:(kc + 1) * TC], [zrg, "xn16"], ["ps_b"],
                       start=(kc == 0), stop=(kc == DTn - 1))
                act(sz16[:, (eb % 3) * TC:(eb % 3 + 1) * TC], ps_b[:, 0:512], AF.Silu,
                    ["ps_b"], [f"sz{eb % 3}"])

            def utw_cpy(eb):
                slp = eb % 2
                cpy("act", utw16[:, slp * 1024:(slp + 1) * 1024], ps_t[:, 0:NADV * 128],
                    ["ps_t0"], [f"utw{slp}"])

            def stage_a(eb):
                slp = eb % 2
                mm(ps_a[:, 0:512],
                   dtw16p[RNK * i:RNK * (i + 1), eb * 128:(eb + 1) * 128],
                   dbcd16[RNK * i:RNK * (i + 1), :], [f"dtw{i}", "dbcd"], ["ps_a"])
                dl = delta[:, slp * TC:(slp + 1) * TC]
                drg = f"delta{slp}"
                sp = spl[:]
                srg = "spl0"
                act(dl, ps_a[:, 0:512], AF.Relu, ["ps_a", f"dtb{i}"], [drg],
                    bias=dtb[i][:, eb:eb + 1])
                act(sp, ps_a[:, 0:512], AF.Abs, ["ps_a", f"dtb{i}"], [srg],
                    bias=dtb[i][:, eb:eb + 1])
                act(sp, sp, AF.Exp, [srg], [srg], scale=-1.0)
                act(sp, sp, AF.Ln, [srg], [srg], bias=1.0)
                tt("dve", dl, dl, sp, ALU.add, [drg, srg], [drg])
                u16s = u16[:, slp * (TC + ADV):(slp + 1) * (TC + ADV)]
                u16rg = f"u16{slp}"
                cpy("pool", u16s[:, 0:ADV],
                    utail[i][:, eb * ADV:(eb + 1) * ADV], [f"utail{i}"], [u16rg])
                tt("dve", u16s[:, ADV:TC + ADV],
                   dl, xb16[:, eb * TC:(eb + 1) * TC], ALU.mult, [drg, "xb16"], [u16rg])
                cpy("pool", utail[i][:, eb * ADV:(eb + 1) * ADV],
                    u16s[:, TC:TC + ADV], [u16rg], [f"utail{i}"])
                for a in range(NADV):
                    transp(ps_t[:, a * 128:(a + 1) * 128],
                           u16s[:, a * ADV: a * ADV + 128],
                           ident16[:],
                           [u16rg, "ident"], ["ps_t0"])

            def stage_b(eb):
                slp = eb % 2
                utws = utw16[:, slp * 1024:(slp + 1) * 1024]
                urg = f"utw{slp}"
                dl = delta[:, slp * TC:(slp + 1) * TC]
                drg = f"delta{slp}"
                for g in range(NST // 2):
                    gsl = g % 2
                    vg = ps_v[:, gsl * 1024:(gsl + 1) * 1024]
                    vrg = f"ps_v{gsl}"
                    for a in range(NADV):
                        for nl in range(2):
                            n = 2 * g + nl
                            mm(vg[:, nl * 512 + a * ADV: nl * 512 + (a + 1) * ADV],
                               utws[:, a * 128:(a + 1) * 128],
                               wbeta[:, a * NST * ADV + n * ADV: a * NST * ADV + (n + 1) * ADV],
                               [urg, f"wbeta{a}"], [vrg])
                    for nl in range(2):
                        n = 2 * g + nl
                        afs = a_f[:, (n % 4) * TC:(n % 4 + 1) * TC]
                        arg = f"a_f{n % 4}"
                        act(afs, dl, AF.Exp, [drg, f"wneg{i}"], [arg],
                            scale=wneg[i][:, eb * NST + n: eb * NST + n + 1])
                        scan(h16[:, n * TC:(n + 1) * TC], afs, vg[:, nl * 512:(nl + 1) * 512],
                             carry[i][:, eb * NST + n: eb * NST + n + 1],
                             [arg, vrg, f"carry{i}"],
                             [f"h{n}"] + (["hf32"] if n >= 8 else []))
                    if g == 3:
                        cpy("dve", carry[i][:, eb * NST:eb * NST + 8],
                            h16[:, 0:8 * TC].rearrange("p (n t) -> p n t", n=8)[:, :, TC - 1],
                            HLO, [f"carry{i}"])
                        tt("dve", h16[:, 0:8 * TC], h16[:, 0:8 * TC], crep[:, 0:8 * TC],
                           ALU.mult, HLO + ["crep", f"carry{i}"], HLO)
                    if g == 5:
                        cpy("dve", carry[i][:, eb * NST + 8:eb * NST + 12],
                            h16[:, 8 * TC:12 * TC].rearrange("p (n t) -> p n t", n=4)[:, :, TC - 1],
                            HHI[0:4], [f"carry{i}"])
                        tt("pool", sq16[:, 0:4 * TC], h16[:, 8 * TC:12 * TC],
                           crep[:, 8 * TC:12 * TC],
                           ALU.mult, HHI[0:4] + ["crep", f"carry{i}"], ["sq16"])
                        if eb > 0:
                            dp_y3(eb - 1)
                            y3_emit(eb - 1)
                cpy("dve", carry[i][:, eb * NST + 12:(eb + 1) * NST],
                    h16[:, 12 * TC:NST * TC].rearrange("p (n t) -> p n t", n=4)[:, :, TC - 1],
                    HHI[4:8], [f"carry{i}"])
                tt("dve", h16[:, 12 * TC:NST * TC], h16[:, 12 * TC:NST * TC],
                   crep[:, 12 * TC:NST * TC], ALU.mult,
                   HHI[4:8] + ["crep", f"carry{i}"], HHI[4:8])
                tt("dve", h16[:, 0:4 * TC], h16[:, 0:4 * TC], sq16[:, 0:4 * TC],
                   ALU.add, HLO + ["sq16"], HLO[0:4])
                tt("dve", h16[:, 4 * TC:8 * TC], h16[:, 4 * TC:8 * TC],
                   h16[:, 12 * TC:NST * TC],
                   ALU.add, HLO[4:8] + HHI[4:8], HLO[4:8])
                if eb + 1 < EBn:
                    utw_cpy(eb + 1)
                    z_jit(eb + 1)
                tt("pool", hf[:, 2048:4096], h16[:, 0:4 * TC], h16[:, 4 * TC:8 * TC],
                   ALU.add, HLO, ["hf32"] + HHI)
                tt("pool", hf[:, 2048:3072], hf[:, 2048:3072], hf[:, 3072:4096],
                   ALU.add, ["hf32"], ["hf32"])
                tt("pool", t1f[:], hf[:, 2048:2560], hf[:, 2560:3072],
                   ALU.add, ["hf32"], ["t1f"])

            for ebi in range(EBn + 1):
                if ebi < EBn:
                    stage_a(ebi)
                    if ebi == 0:
                        utw_cpy(0)
                        z_jit(0)
                if ebi >= 1:
                    stage_b(ebi - 1)
            dp_y3(EBn - 1)
            y3_emit(EBn - 1)

            if phase < 5:
                return
            for mt in range(DTn):
                slp = mt % 2
                wsl3 = wB[:, slp * EBn * 128:(slp + 1) * EBn * 128]
                org = f"wB{slp}"
                dsp(wsl3, scr[f"L{i}_wo"][mt], [f"swo{i}"], [org])
                pps = ps_a if slp == 0 else ps_b
                psrg = "ps_a" if slp == 0 else "ps_b"
                for kc in range(EBn):
                    mm(pps[:, 0:512], wsl3[:, kc * 128:(kc + 1) * 128],
                       y3a[:, kc * TC:(kc + 1) * TC], [org, "y3a"], [psrg],
                       start=(kc == 0), stop=(kc == EBn - 1))
                tt("dve", rres[:, mt * TC:(mt + 1) * TC], rres[:, mt * TC:(mt + 1) * TC],
                   pps[:, 0:512], ALU.add, ["rres", psrg], ["rres"])

        def emit_chunk(k):
            dsp(x0[:].rearrange("p (m t) -> p m t", m=DTn),
                xT_d[:, k * TC:(k + 1) * TC].rearrange("(m p) t -> p m t", p=128),
                [], ["x0"])
            ln_stats(x0[:], "x0", True)
            ln_rows(False)
            ln_apply_plain(x0[:], "x0", rres[:], "rres")
            for i in range(2):
                emit_mamba(k, i)
            if phase < 6:
                dsp(yT_d[:, k * TC:(k + 1) * TC].rearrange("(m p) t -> p m t", p=128),
                    x0[:].rearrange("p (m t) -> p m t", m=DTn), ["x0"], ["yout"])
                return
            ln_stats(rres[:], "rres", True)
            ln_rows(False)
            ln_apply_plain(rres[:], "rres", None, None, add_to=x0[:], add_rg="x0")
            ln_stats(x0[:], "x0", True)
            ln_rows(False)
            xlo_tiles = [
                (delta[:].bitcast(BF16)[:, 0:512], "delta0"),
                (delta[:].bitcast(BF16)[:, 1024:1536], "delta1"),
                (a_f[:].bitcast(BF16)[:, 0:512], "a_f0"),
                (a_f[:].bitcast(BF16)[:, 1024:1536], "a_f1"),
                (a_f[:].bitcast(BF16)[:, 2048:2560], "a_f2"),
                (a_f[:].bitcast(BF16)[:, 3072:3584], "a_f3"),
                (spl[:].bitcast(BF16)[:, 0:512], "spl0"),
                (sz16[:, 0:512], "sz0"),
            ]
            tt("dve", tmpr[:], mr[:], rstdr[:], ALU.mult, ["mr", "rstdr"], ["tmpr"])
            hilo_row(rs16, "rs16", rstdr[:], "rstdr")
            hilo_row(mm16, "mm16", tmpr[:], "tmpr")
            bcast2(ps_s[:, 0:512], "ps_s", rs16, "rs16")
            bcast2(ps_b[:, 0:512], "ps_b", mm16, "mm16")
            for mt in range(DTn):
                sl = slice(mt * TC, (mt + 1) * TC)
                tt("dve", t1f[:], x0[:, sl], ps_s[:, 0:512], ALU.mult,
                   ["x0", "ps_s"], ["t1f"])
                tt("dve", t1f[:], t1f[:], ps_b[:, 0:512], ALU.subtract,
                   ["t1f", "ps_b"], ["t1f"])
                cpy("dve", xn16[:, sl], t1f[:], ["t1f"], ["xn16"])
                lo_ap, lo_rg = xlo_tiles[mt]
                tt("dve", lo_ap, t1f[:], xn16[:, sl], ALU.subtract,
                   ["t1f", "xn16"], [lo_rg])
            ffns = wbeta[:].bitcast(F32)
            pend = []
            for a in range(NADV):
                t = P.wmap.get(f"wbeta{a}")
                if t:
                    pend.append(t)
                pend.extend(P.rmap.get(f"wbeta{a}", ()))
            P.rmap.setdefault("ffns", []).extend(pend)
            for mt in range(DTn):
                slw = mt % 2
                wsl = wA[:, slw * DTn * 128:(slw + 1) * DTn * 128]
                wrg = f"wA{slw}"
                wsl2b = wB[:, slw * EBn * 128: slw * EBn * 128 + DTn * 128]
                wrg2 = f"wB{slw}"
                dsp(wsl, scr["m"][mt], ["scrm"], [wrg])
                dsp(wsl2b, scr["ml"][mt], ["scrm"], [wrg2])
                pps = ps_a if mt % 2 == 0 else ps_b
                psrg = "ps_a" if mt % 2 == 0 else "ps_b"
                for kc in range(DTn):
                    lo_ap, lo_rg = xlo_tiles[kc]
                    mm(pps[:, 0:512], wsl[:, kc * 128:(kc + 1) * 128],
                       xn16[:, kc * TC:(kc + 1) * TC], [wrg, "xn16"], [psrg],
                       start=(kc == 0), stop=False)
                    mm(pps[:, 0:512], wsl[:, kc * 128:(kc + 1) * 128],
                       lo_ap, [wrg, lo_rg], [psrg], start=False, stop=False)
                    mm(pps[:, 0:512], wsl2b[:, kc * 128:(kc + 1) * 128],
                       xn16[:, kc * TC:(kc + 1) * TC], [wrg2, "xn16"], [psrg],
                       start=False, stop=(kc == DTn - 1))
                tt("dve", ffns[:, mt * TC:(mt + 1) * TC], pps[:, 0:512],
                   fbias[:, mt:mt + 1].broadcast_to([128, TC]), ALU.add,
                   [psrg, "fbias"], ["ffns"])
            ln_stats(ffns[:, 0:DTn * TC], "ffns", True)
            ln_rows(False)
            ln_apply_plain(ffns[:, 0:DTn * TC], "ffns", None, None, add_to=x0[:], add_rg="x0")
            for a in range(NADV):
                if "ffns" in P.wmap:
                    P.wmap[f"wbeta{a}"] = P.wmap["ffns"]
                P.rmap[f"wbeta{a}"] = list(P.rmap.get("ffns", ()))
            dsp(yT_d[:, k * TC:(k + 1) * TC].rearrange("(m p) t -> p m t", p=128),
                x0[:].rearrange("p (m t) -> p m t", m=DTn), ["x0"], ["yout"])

        def xfer(dst, srcrg):
            if srcrg in P.wmap:
                P.wmap[dst] = P.wmap[srcrg]
            P.rmap[dst] = list(P.rmap.get(srcrg, ()))
        xfer("xb16", "trh")
        xfer("xn16", "trl")
        xfer("sq16", "trl")
        xfer("dpx", "trl")
        xfer("crep", "tih")
        for a in range(NADV):
            xfer(f"wbeta{a}", "til")
        for n2 in range(NST):
            xfer(f"h{n2}", "h16")
        xfer("y3a", "mstg")
        P.rmap.setdefault("y3a", []).extend(P.rmap.get("y3aW", ()))
        for s2 in ("a", "b", "c", "d"):
            for dstrg in ("delta0", "delta1"):
                xfer(dstrg, "cs0" + s2)
            for dstrg in ("a_f0", "a_f1"):
                xfer(dstrg, "cs1" + s2)
        for k in range(NCH):
            emit_chunk(k)

        if dbg:
            dsp(mdbg_d, scr["m"][:, :], ["scrm"], ["mdbg"])
            P.barrier("sp", ["mdbg"])
        P.barrier("sp", ["yout"])

        def mk_emitter(engname):
            def emitter(eng):
                hw = {s: 0 for s in sems}
                for entry in P.ops[engname]:
                    typ, waits, fn, slot, seq = entry[0], entry[1], entry[2], entry[3], entry[4]
                    qn = entry[5] if len(entry) > 5 else None
                    for (src, val) in waits:
                        v = val * (16 if "D" in src else 1)
                        if src == engname == "pe":
                            continue
                        if v > hw[src]:
                            eng.wait_ge(sems[src], v)
                            hw[src] = v
                    if typ == "wait":
                        continue
                    if typ == "op":
                        inst = fn(eng)
                        inst.then_inc(sems[engname], 1)
                    else:
                        sn = f"{qn}D{slot}"
                        if seq > 1:
                            v = 16 * (seq - 1)
                            if v > hw[sn]:
                                eng.wait_ge(sems[sn], v)
                                hw[sn] = v
                        inst = fn(eng)
                        inst.then_inc(sems[sn], 16)
            return emitter

        block.tensor(mk_emitter("pe"))
        block.vector(mk_emitter("dve"))
        block.scalar(mk_emitter("act"))
        block.gpsimd(mk_emitter("pool"))
        block.sync(mk_emitter("sp"))

    return nc


_NC_CACHE = {}


def prep_common(inputs):
    def f32c(a):
        return np.ascontiguousarray(np.asarray(a, dtype=np.float32))

    common = {}
    for i in range(2):
        p = f"L{i}_"
        inw = f32c(inputs["L0_attn_in_proj_w"][i])
        common[p + "inw_xbT"] = f32c(inw[:E].T)
        common[p + "inw_zT"] = f32c(inw[E:].T)
        common[p + "convw"] = f32c(np.transpose(
            inputs["L0_attn_conv_w"][i].reshape(EBn, 128, DCONV), (1, 0, 2)).reshape(128, EBn * DCONV))
        common[p + "convb"] = f32c(np.transpose(
            np.asarray(inputs["L0_attn_conv_b"][i]).reshape(EBn, 128)))
        xp = f32c(inputs["L0_attn_x_proj_w"][i])
        def tile_e_rows(arr):
            r = arr.shape[1]
            return f32c(np.transpose(arr.reshape(EBn, 128, r), (1, 0, 2)).reshape(128, EBn * r))
        common[p + "xpdT"] = tile_e_rows(np.asarray(xp[0:RNK].T))
        common[p + "xpBT"] = tile_e_rows(np.asarray(xp[RNK:RNK + NST].T))
        common[p + "xpCT"] = tile_e_rows(np.asarray(xp[RNK + NST:RNK + 2 * NST].T))
        common[p + "dtwT"] = f32c(inputs["L0_attn_dt_w"][i].T)
        common[p + "dtb"] = f32c(np.transpose(
            np.asarray(inputs["L0_attn_dt_b"][i]).reshape(EBn, 128)))
        common[p + "alog"] = tile_e_rows(np.asarray(inputs["L0_attn_A_log"][i]))
        common[p + "dp"] = f32c(np.transpose(
            np.asarray(inputs["L0_attn_Dp"][i]).reshape(EBn, 128)))
        common[p + "owT"] = f32c(inputs["L0_attn_out_w"][i].T)
        common[p + "normw"] = f32c(np.transpose(
            np.asarray(inputs["L0_attn_norm_w"][i]).reshape(DTn, 128)))
    common["Wr"] = f32c(inputs["L0_ffn_w_freq_real"])
    common["Wi"] = f32c(inputs["L0_ffn_w_freq_imag"])
    common["fbias"] = f32c(np.transpose(
        np.asarray(inputs["L0_ffn_bias"]).reshape(DTn, 128)))

    common["ident16"] = np.eye(128, dtype=BF)
    jj = np.arange(ADV)[None, :]
    rr = np.arange(128)[:, None]
    bp = np.where(rr <= 64 + jj, BETA ** ((64 + jj) - rr), 0.0).astype(np.float32)
    common["betap16"] = bp
    common["onesrow16"] = np.ones((1, 128), dtype=BF)
    common["constrow"] = np.tile(np.array([[EPS, 1.0, -1.0, -0.5, 1.0 / D, 0.0, 2.0, 0.5]],
                                           dtype=np.float32), (128, 1))
    common["onescol16"] = np.ones((128, 1), dtype=BF)
    sel = np.zeros((NST, NST * 128), dtype=np.float32)
    for n in range(NST):
        sel[n, n * 128:(n + 1) * 128] = 1.0
    common["sel16"] = sel.astype(BF)
    grid = np.arange(D)
    ang = 2.0 * np.pi * np.outer(grid, grid) / D
    cf = np.cos(ang).astype(np.float32)
    sf = np.sin(ang).astype(np.float32)
    def cstile(a):
        return np.ascontiguousarray(
            np.transpose(a.reshape(DTn, 128, 2, 512), (0, 2, 1, 3)).reshape(DTn * 2, 128, 512))
    for nm, arr in (("Cf", cf), ("Sf", sf), ("Sfn", -sf)):
        hi = arr.astype(BF)
        lo = (arr - hi.astype(np.float32)).astype(BF)
        common[nm + "_hi"] = cstile(hi)
        common[nm + "_lo"] = cstile(lo)
    return common


def kernel(**inputs):
    x = np.asarray(inputs["L0_attn_x"], dtype=np.float32)

    def f32c(a):
        return np.ascontiguousarray(np.asarray(a, dtype=np.float32))

    common = prep_common(inputs)

    if "nc" not in _NC_CACHE:
        _NC_CACHE["nc"] = build_nc()
    nc = _NC_CACHE["nc"]

    in_maps = []
    for c in range(NCORES):
        b = c // 2
        m = dict(common)
        m["xT"] = f32c(x[b].T)
        in_maps.append(m)

    res = run_bass_kernel_spmd(nc, in_maps, core_ids=list(range(NCORES)))
    out = np.empty((B, L, D), dtype=np.float32)
    for b in range(B):
        out[b] = np.asarray(res.results[2 * b]["yT"]).T
    return out


# version 10
# speedup vs baseline: 1.2522x; 1.2522x over previous
import numpy as np
import ml_dtypes
import concourse.bass as bass
import concourse.mybir as mybir
from concourse.bass_utils import run_bass_kernel_spmd
from contextlib import ExitStack

F32 = mybir.dt.float32
BF16 = mybir.dt.bfloat16
AF = mybir.ActivationFunctionType
ALU = mybir.AluOpType

B, LFULL, D = 4, 4096, 1024
import os as _os
L = int(_os.environ.get("MK_L", str(LFULL)))
E, NST, RNK, DCONV = 2048, 16, 64, 4
BETA, EPS = 0.6, 1e-5
TC = 512
NCH = L // TC
EBn = E // 128
DTn = D // 128
ADV = 64
NADV = TC // ADV
NCORES = 8
BF = ml_dtypes.bfloat16


class Prog:

    ENGS = ("pe", "dve", "act", "pool", "sp")

    QMAP = {"sp": "sp", "pool": "pool", "sc": "act"}

    def __init__(self):
        self.ops = {e: [] for e in self.ENGS}
        self.cnt = {e: 0 for e in self.ENGS}
        self.dcnt = {"sp": 0, "pool": 0, "sc": 0}
        self.wmap = {}
        self.rmap = {}

    def _deps(self, reads, writes):
        waits = []
        for rg in reads:
            t = self.wmap.get(rg)
            if t:
                waits.append(t)
        for rg in writes:
            t = self.wmap.get(rg)
            if t:
                waits.append(t)
            waits.extend(self.rmap.get(rg, ()))
        return waits

    def _commit(self, tok, reads, writes):
        for rg in reads:
            self.rmap.setdefault(rg, []).append(tok)
        for rg in writes:
            self.wmap[rg] = tok
            self.rmap[rg] = []

    def op(self, eng, fn, reads=(), writes=(), extra=()):
        waits = self._deps(reads, writes) + [w for w in extra if w]
        self.cnt[eng] += 1
        tok = (eng, self.cnt[eng])
        self.ops[eng].append(("op", tuple(waits), fn, None, None))
        self._commit(tok, reads, writes)
        return tok

    NSLOT = 8

    def dma(self, q, fn, reads=(), writes=(), extra=()):
        eng = self.QMAP[q]
        waits = self._deps(reads, writes) + [w for w in extra if w]
        idx = self.dcnt[q]
        self.dcnt[q] += 1
        slot = idx % self.NSLOT
        seq = idx // self.NSLOT + 1
        tok = (f"{q}D{slot}", seq)
        self.ops[eng].append(("dma", tuple(waits), fn, slot, seq, q))
        self._commit(tok, reads, writes)
        return tok

    def barrier(self, eng, reads):
        waits = self._deps(reads, ())
        self.ops[eng].append(("wait", tuple(waits), None, None, None))


def build_nc(Lb=None, dbg=False, phase=9, sub=9):
    global L, NCH
    if Lb is not None:
        L = Lb
        NCH = L // TC
    nc = bass.Bass("TRN2", target_bir_lowering=False)

    xT_d = nc.dram_tensor("xT", [D, L], F32, kind="ExternalInput").ap()
    lw = {}
    for i in range(2):
        p = f"L{i}_"
        for nm, shp in (("inw_xbT", [D, E]), ("inw_zT", [D, E]), ("convw", [128, EBn * DCONV]),
                        ("convb", [128, EBn]), ("xpdT", [128, EBn * RNK]), ("xpBT", [128, EBn * NST]),
                        ("xpCT", [128, EBn * NST]), ("dtwT", [RNK, E]), ("dtb", [128, EBn]),
                        ("alog", [128, EBn * NST]), ("dp", [128, EBn]), ("owT", [E, D]),
                        ("normw", [128, DTn])):
            lw[p + nm] = nc.dram_tensor(p + nm, shp, F32, kind="ExternalInput").ap()
    wr_d = nc.dram_tensor("Wr", [D, D], F32, kind="ExternalInput").ap()
    wi_d = nc.dram_tensor("Wi", [D, D], F32, kind="ExternalInput").ap()
    fbias_d = nc.dram_tensor("fbias", [128, DTn], F32, kind="ExternalInput").ap()
    ident16_d = nc.dram_tensor("ident16", [128, 128], BF16, kind="ExternalInput").ap()
    betap_d = nc.dram_tensor("betap16", [128, ADV], F32, kind="ExternalInput").ap()
    onesrow_d = nc.dram_tensor("onesrow16", [1, 128], BF16, kind="ExternalInput").ap()
    onescol_d = nc.dram_tensor("onescol16", [128, 1], BF16, kind="ExternalInput").ap()
    sel16_d = nc.dram_tensor("sel16", [NST, NST * 128], BF16, kind="ExternalInput").ap()
    cfh_d = nc.dram_tensor("Cf_hi", [DTn * 2, 128, 512], BF16, kind="ExternalInput").ap()
    cfl_d = nc.dram_tensor("Cf_lo", [DTn * 2, 128, 512], BF16, kind="ExternalInput").ap()
    sfh_d = nc.dram_tensor("Sf_hi", [DTn * 2, 128, 512], BF16, kind="ExternalInput").ap()
    sfl_d = nc.dram_tensor("Sf_lo", [DTn * 2, 128, 512], BF16, kind="ExternalInput").ap()
    snh_d = nc.dram_tensor("Sfn_hi", [DTn * 2, 128, 512], BF16, kind="ExternalInput").ap()
    snl_d = nc.dram_tensor("Sfn_lo", [DTn * 2, 128, 512], BF16, kind="ExternalInput").ap()
    crow_d = nc.dram_tensor("constrow", [128, 8], F32, kind="ExternalInput").ap()
    yT_d = nc.dram_tensor("yT", [D, L], F32, kind="ExternalOutput").ap()
    mdbg_d = nc.dram_tensor("m_dbg", [D, D], BF16, kind="ExternalOutput").ap() if dbg else None

    scr = {}
    for i in range(2):
        p = f"L{i}_"
        scr[p + "wxb"] = nc.dram_tensor(p + "s_wxb", [EBn, 128, DTn * 128], BF16)
        scr[p + "wz"] = nc.dram_tensor(p + "s_wz", [EBn, 128, DTn * 128], BF16)
        scr[p + "wo"] = nc.dram_tensor(p + "s_wo", [DTn, 128, EBn * 128], BF16)
    scr["m"] = nc.dram_tensor("s_m", [DTn, 128, DTn * 128], BF16)
    scr["ml"] = nc.dram_tensor("s_ml", [DTn, 128, DTn * 128], BF16)
    for wnm in ("wrh", "wrl", "wih", "wil"):
        scr[wnm] = nc.dram_tensor("s_" + wnm, [D, D], BF16)

    P = Prog()

    with ExitStack() as ctx:
        en = ctx.enter_context

        ident16 = en(nc.sbuf_tensor("ident16_s", [128, 128], BF16))
        betap16 = en(nc.sbuf_tensor("betap16_s", [128, ADV], F32))
        onesrow16 = en(nc.sbuf_tensor("onesrow16_s", [1, 128], BF16))
        onescol16 = en(nc.sbuf_tensor("onescol16_s", [128, 1], BF16))
        sel16 = en(nc.sbuf_tensor("sel16_s", [NST, NST * 128], BF16))
        xpd16 = [en(nc.sbuf_tensor(f"xpd16_{i}", [128, EBn * RNK], BF16)) for i in range(2)]
        xpB16 = [en(nc.sbuf_tensor(f"xpB16_{i}", [128, EBn * NST], BF16)) for i in range(2)]
        xpC16 = [en(nc.sbuf_tensor(f"xpC16_{i}", [128, EBn * NST], BF16)) for i in range(2)]
        dtw16p = en(nc.sbuf_tensor("dtw16p", [128, E], BF16))
        convb = [en(nc.sbuf_tensor(f"convb_{i}", [128, EBn], F32)) for i in range(2)]
        cwsb = [en(nc.sbuf_tensor(f"cwsb_{i}", [128, EBn * DCONV], F32)) for i in range(2)]
        dtb = [en(nc.sbuf_tensor(f"dtb_{i}", [128, EBn], F32)) for i in range(2)]
        dpv = [en(nc.sbuf_tensor(f"dp_{i}", [128, EBn], F32)) for i in range(2)]
        normw = [en(nc.sbuf_tensor(f"normw_{i}", [128, DTn], F32)) for i in range(2)]
        wneg = [en(nc.sbuf_tensor(f"wneg_{i}", [128, EBn * NST], F32)) for i in range(2)]
        fbias = en(nc.sbuf_tensor("fbias_s", [128, DTn], F32))

        x0 = en(nc.sbuf_tensor("x0", [128, DTn * TC], F32))
        rres = en(nc.sbuf_tensor("rres", [128, DTn * TC], F32))
        big16 = en(nc.sbuf_tensor("big16", [128, 2 * DTn * TC], BF16))
        xn16 = big16[:, 0:DTn * TC]
        sq16 = big16[:, DTn * TC:2 * DTn * TC]
        xzr = en(nc.sbuf_tensor("xzr", [128, 2 * (TC + 3)], F32))
        xztail = [en(nc.sbuf_tensor(f"xztail_{i}", [128, EBn * 3], F32)) for i in range(2)]
        xb16 = en(nc.sbuf_tensor("xb16", [128, EBn * TC], BF16))
        u16 = en(nc.sbuf_tensor("u16", [128, 2 * (TC + ADV)], BF16))
        utail = [en(nc.sbuf_tensor(f"utail_{i}", [128, EBn * ADV], BF16)) for i in range(2)]
        b16 = en(nc.sbuf_tensor("b16", [NST, TC + ADV], BF16))
        btail = [en(nc.sbuf_tensor(f"btail_{i}", [NST, ADV], BF16)) for i in range(2)]
        c16 = en(nc.sbuf_tensor("c16", [NST, TC], BF16))
        dbcd16 = en(nc.sbuf_tensor("dbcd16", [128, TC], BF16))
        btw16 = en(nc.sbuf_tensor("btw16", [128, NADV * NST], BF16))
        utw16 = en(nc.sbuf_tensor("utw16", [128, 2 * NADV * 128], BF16))
        wbeta = en(nc.sbuf_tensor("wbeta", [128, NADV * NST * ADV], BF16))
        delta = en(nc.sbuf_tensor("delta", [128, 2 * TC], F32))
        spl = en(nc.sbuf_tensor("spl", [128, TC], F32))
        a_f = en(nc.sbuf_tensor("a_f", [128, 4 * TC], F32))
        h16 = en(nc.sbuf_tensor("h16", [128, NST * TC], BF16))
        crep = en(nc.sbuf_tensor("crep", [128, NST * TC], BF16))
        t1f = en(nc.sbuf_tensor("t1f", [128, TC], F32))
        sz16 = en(nc.sbuf_tensor("sz16", [128, 3 * TC], BF16))
        y3a = en(nc.sbuf_tensor("y3a", [128, EBn * TC], BF16))
        carry = [en(nc.sbuf_tensor(f"carry_{i}", [128, EBn * NST], BF16)) for i in range(2)]
        wA = en(nc.sbuf_tensor("wA", [128, 2 * DTn * 128], BF16))
        wB = en(nc.sbuf_tensor("wB", [128, 2 * EBn * 128], BF16))
        crow = en(nc.sbuf_tensor("crow", [128, 8], F32))
        mr = en(nc.sbuf_tensor("mr", [1, TC], F32))
        rstdr = en(nc.sbuf_tensor("rstdr", [1, TC], F32))
        tmpr = en(nc.sbuf_tensor("tmpr", [1, TC], F32))
        mm16 = en(nc.sbuf_tensor("mm16", [1, 2 * TC], BF16))
        rs16 = en(nc.sbuf_tensor("rs16", [1, 2 * TC], BF16))

        ps_a = en(nc.psum_tensor("ps_a", [128, 512], F32))
        ps_b = en(nc.psum_tensor("ps_b", [128, 512], F32))
        ps_v = en(nc.psum_tensor("ps_v", [128, 2048], F32))
        ps_t = en(nc.psum_tensor("ps_t", [128, 1024], BF16))
        ps_s = en(nc.psum_tensor("ps_s", [128, 512], F32))

        semnames = ["pe", "dve", "act", "pool", "sp"]
        for q in ("sp", "pool", "sc"):
            for s in range(Prog.NSLOT):
                semnames.append(f"{q}D{s}")
        sems = {n: en(nc.semaphore("sem_" + n)) for n in semnames}

        block = en(nc.Block())

        _CONSTS = [EPS, 1.0, -1.0, -0.5, 1.0 / D, 0.0, 2.0, 0.5]
        for _ci, _cv in enumerate(_CONSTS):
            nc.const_aps.register(F32, _cv, crow[:, _ci:_ci + 1])

        def mm(out, lhsT, rhs, reads, writes, start=True, stop=True, extra=()):
            return P.op("pe", lambda t, o=out, l=lhsT, r=rhs, s0=start, s1=stop:
                        t.matmul(o, l, r, start=s0, stop=s1), reads, writes, extra)

        def transp(out, in_, ident_ap, reads, writes):
            return P.op("pe", lambda t, o=out, i=in_, d=ident_ap:
                        t.transpose(o, i, d), reads, writes)

        def act(out, in_, func, reads, writes, bias=0.0, scale=1.0):
            rd = list(reads)
            if isinstance(bias, float) or isinstance(scale, float):
                rd.append("crow")
            return P.op("act", lambda s, o=out, i=in_, f=func, b=bias, sc=scale:
                        s.activation(o, i, f, bias=b, scale=sc), rd, writes)

        def tt(eng, out, i0, i1, op, reads, writes):
            return P.op(eng, lambda v, o=out, a0=i0, a1=i1, p=op:
                        v.tensor_tensor(o, a0, a1, p), reads, writes)

        def tsc(eng, out, i0, sc1, sc2, op0, op1, reads, writes):
            return P.op(eng, lambda v, o=out, a0=i0, x=sc1, y=sc2, p0=op0, p1=op1:
                        v.tensor_scalar(o, a0, x, y, p0, p1), reads, writes)

        def stt(eng, out, i0, sc, i1, op0, op1, reads, writes):
            return P.op(eng, lambda v, o=out, a0=i0, x=sc, a1=i1, p0=op0, p1=op1:
                        v.scalar_tensor_tensor(o, a0, x, a1, p0, p1), reads, writes)

        def cpy(eng, out, in_, reads, writes):
            if eng == "act":
                return P.op(eng, lambda v, o=out, i=in_: v.copy(o, i), reads, writes)
            return P.op(eng, lambda v, o=out, i=in_: v.tensor_copy(o, i), reads, writes)

        def scan(out, d0, d1, init, reads, writes):
            return P.op("dve", lambda v, o=out, a0=d0, a1=d1, ii=init:
                        v.tensor_tensor_scan(o, a0, a1, ii, ALU.mult, ALU.add),
                        reads, writes)

        def mset(eng, ap_, val, reads, writes):
            return P.op(eng, lambda v, a0=ap_, x=val: v.memset(a0, x), reads, writes)

        def dsp(out, in_, reads, writes):
            return P.dma("sp", lambda s, o=out, i=in_: s.dma_start(out=o, in_=i),
                         reads, writes)

        def dgp(out, in_, reads, writes):
            return P.dma("pool", lambda g, o=out, i=in_: g.dma_start(out=o, in_=i),
                         reads, writes)

        def dsc(out, in_, reads, writes):
            return P.dma("sc", lambda s, o=out, i=in_: s.dma_start(out=o, in_=i),
                         reads, writes)

        _dalt = [0]

        def dwt(out, in_, reads, writes):
            _dalt[0] ^= 1
            return (dsp if _dalt[0] else dsc)(out, in_, reads, writes)

        dsp(crow[:], crow_d, [], ["crow"])
        dsp(ident16[:], ident16_d, [], ["ident"])
        dsp(betap16[:], betap_d, [], ["betap"])
        dsp(onesrow16[:], onesrow_d, [], ["onesrow"])
        dsp(onescol16[:], onescol_d, [], ["onescol"])
        dsp(sel16[:], sel16_d, [], ["sel"])
        dsp(fbias[:], fbias_d, [], ["fbias"])
        for i in range(2):
            p = f"L{i}_"
            dgp(xpd16[i][:], lw[p + "xpdT"], [], [f"xpd{i}"])
            dgp(xpB16[i][:], lw[p + "xpBT"], [], [f"xpB{i}"])
            dgp(xpC16[i][:], lw[p + "xpCT"], [], [f"xpC{i}"])
            dgp(dtw16p[RNK * i:RNK * (i + 1), :], lw[p + "dtwT"], [], [f"dtw{i}"])
            dsp(convb[i][:], lw[p + "convb"], [], [f"convb{i}"])
            dsp(cwsb[i][:], lw[p + "convw"], [], [f"cw{i}"])
            dsp(dtb[i][:], lw[p + "dtb"], [], [f"dtb{i}"])
            dsp(dpv[i][:], lw[p + "dp"], [], [f"dp{i}"])
            dsp(normw[i][:], lw[p + "normw"], [], [f"normw{i}"])
            dsp(wneg[i][:], lw[p + "alog"], [], [f"wneg{i}"])
            act(wneg[i][:], wneg[i][:], AF.Exp, [f"wneg{i}"], [f"wneg{i}"])
            tsc("dve", wneg[i][:], wneg[i][:], -1.0, None, ALU.mult, ALU.bypass,
                [f"wneg{i}"], [f"wneg{i}"])
            for _eb in range(EBn):
                dgp(scr[p + "wxb"].ap().rearrange("eb p (kc c) -> eb p kc c", c=128)[_eb],
                    lw[p + "inw_xbT"].rearrange("(kc p) (eb c) -> eb p kc c", p=128, c=128)[_eb],
                    [], [f"swxb{i}"])
                dgp(scr[p + "wz"].ap().rearrange("eb p (kc c) -> eb p kc c", c=128)[_eb],
                    lw[p + "inw_zT"].rearrange("(kc p) (eb c) -> eb p kc c", p=128, c=128)[_eb],
                    [], [f"swz{i}"])
            for _mt in range(DTn):
                dgp(scr[p + "wo"].ap().rearrange("mt p (kc c) -> mt p kc c", c=128)[_mt],
                    lw[p + "owT"].rearrange("(kc p) (mt c) -> mt p kc c", p=128, c=128)[_mt],
                    [], [f"swo{i}"])
            mset("dve", carry[i][:], 0.0, [], [f"carry{i}"])
            mset("dve", utail[i][:], 0.0, [], [f"utail{i}"])
            mset("dve", btail[i][:], 0.0, [], [f"btail{i}"])
            mset("dve", xztail[i][:], 0.0, [], [f"xztail{i}"])

        wstg_h = big16[:, 0:D]
        wstg_l = big16[:, D:2 * D]
        wstg_f = xb16[:, 0:2 * D].bitcast(F32)
        for (wd, hnm, lnm) in ((wr_d, "wrh", "wrl"), (wi_d, "wih", "wil")):
            for kc in range(DTn):
                dsp(wstg_f, wd[kc * 128:(kc + 1) * 128, :], ["wsl"], ["wsf"])
                cpy("act", wstg_h, wstg_f, ["wsf"], ["wsh"])
                tt("dve", wstg_l, wstg_f, wstg_h, ALU.subtract, ["wsf", "wsh"], ["wsl"])
                dsp(scr[hnm][kc * 128:(kc + 1) * 128, :], wstg_h, ["wsh"], ["scrw"])
                dsp(scr[lnm][kc * 128:(kc + 1) * 128, :], wstg_l, ["wsl"], ["scrw"])

        trh, trl = xb16[:], big16[:]
        tih, til = crep[:], wbeta[:]
        cstr0 = delta[:].bitcast(BF16)
        cstr1 = a_f[:, 0:2 * TC].bitcast(BF16)

        def load_cs(slot, kc, nh, s_hi_d, s_lo_d):
            cs = cstr0 if slot == 0 else cstr1
            rg = f"cs{slot}"
            t = kc * 2 + nh
            dsp(cs[:, 0:512], cfh_d[t], [], [rg + "a"])
            dsc(cs[:, 512:1024], cfl_d[t], [], [rg + "b"])
            dsp(cs[:, 1024:1536], s_hi_d[t], [], [rg + "c"])
            dsc(cs[:, 1536:2048], s_lo_d[t], [], [rg + "d"])
            return cs

        wsl_rh = y3a[:, 0:D]
        wsl_rl = y3a[:, D:2 * D]
        wsl_ih = y3a[:, 2 * D:3 * D]
        wsl_il = y3a[:, 3 * D:4 * D]
        for ot in range(DTn):
            for (nmh, sl_ap) in (("wrh", wsl_rh), ("wrl", wsl_rl), ("wih", wsl_ih), ("wil", wsl_il)):
                dwt(sl_ap.rearrange("p (kc c) -> p kc c", kc=DTn),
                    scr[nmh][:, ot * 128:(ot + 1) * 128].rearrange("(kc p) c -> p kc c", p=128),
                    ["scrw", "y3aW"], ["y3aW"])
            for pass_ti in (0, 1):
                dsth = trh if pass_ti == 0 else tih
                dstl = trl if pass_ti == 0 else til
                hi_rg = "trh" if pass_ti == 0 else "tih"
                lo_rg = "trl" if pass_ti == 0 else "til"
                s_hi, s_lo = (sfh_d, sfl_d) if pass_ti == 0 else (snh_d, snl_d)
                wa_h, wa_l = (wsl_rh, wsl_rl) if pass_ti == 0 else (wsl_ih, wsl_il)
                wb_h, wb_l = (wsl_ih, wsl_il) if pass_ti == 0 else (wsl_rh, wsl_rl)
                for nh in range(2):
                    gslot = (ot * 4 + pass_ti * 2 + nh) % 2
                    psrg = "ps_a" if gslot == 0 else "ps_b"
                    pps = ps_a if gslot == 0 else ps_b
                    first = True
                    for kc in range(DTn):
                        slot = kc % 2
                        cs = load_cs(slot, kc, nh, s_hi, s_lo)
                        rg = f"cs{slot}"
                        ksl = slice(kc * 128, (kc + 1) * 128)
                        combos = ((wa_h[:, ksl], cs[:, 0:512], rg + "a"),
                                  (wa_h[:, ksl], cs[:, 512:1024], rg + "b"),
                                  (wa_l[:, ksl], cs[:, 0:512], rg + "a"),
                                  (wb_h[:, ksl], cs[:, 1024:1536], rg + "c"),
                                  (wb_h[:, ksl], cs[:, 1536:2048], rg + "d"),
                                  (wb_l[:, ksl], cs[:, 1024:1536], rg + "c"))
                        for ci, (wop, cop, crg) in enumerate(combos):
                            mm(pps[:, 0:512], wop, cop, ["y3aW", crg], [psrg],
                               start=first, stop=(kc == DTn - 1 and ci == 5))
                            first = False
                    dst_sl = slice(ot * D + nh * 512, ot * D + (nh + 1) * 512)
                    act(dsth[:, dst_sl], pps[:, 0:512], AF.Copy, [psrg], [hi_rg])
                    tt("dve", dstl[:, dst_sl], pps[:, 0:512], dsth[:, dst_sl],
                       ALU.subtract, [psrg, hi_rg], [lo_rg])

        for mt in range(DTn):
            for nh in range(2):
                gslot = (mt * 2 + nh) % 2
                psrg = "ps_a" if gslot == 0 else "ps_b"
                pps = ps_a if gslot == 0 else ps_b
                first = True
                for kc in range(DTn):
                    slot = kc % 2
                    cs = load_cs(slot, kc, nh, snh_d, snl_d)
                    rg = f"cs{slot}"
                    sl_k = slice(kc * D + mt * 128, kc * D + (mt + 1) * 128)
                    mm(pps[:, 0:512], trh[:, sl_k], cs[:, 0:512], ["trh", rg + "a"], [psrg],
                       start=first, stop=False)
                    first = False
                    mm(pps[:, 0:512], trh[:, sl_k], cs[:, 512:1024], ["trh", rg + "b"], [psrg],
                       start=False, stop=False)
                    mm(pps[:, 0:512], trl[:, sl_k], cs[:, 0:512], ["trl", rg + "a"], [psrg],
                       start=False, stop=False)
                    mm(pps[:, 0:512], tih[:, sl_k], cs[:, 1024:1536], ["tih", rg + "c"], [psrg],
                       start=False, stop=False)
                    mm(pps[:, 0:512], tih[:, sl_k], cs[:, 1536:2048], ["tih", rg + "d"], [psrg],
                       start=False, stop=False)
                    mm(pps[:, 0:512], til[:, sl_k], cs[:, 1024:1536], ["til", rg + "c"], [psrg],
                       start=False, stop=(kc == DTn - 1))
                act(y3a[:, 0:512], pps[:, 0:512], AF.Copy, [psrg], ["mstg"],
                    scale=1.0 / D)
                stt("dve", y3a[:, 512:1024], pps[:, 0:512], 1.0 / D, y3a[:, 0:512],
                    ALU.mult, ALU.subtract, [psrg, "mstg"], ["mstg"])
                mdst = scr["m"].ap().rearrange("mo p (kc c) -> mo p kc c", c=128)[
                    nh * 4:(nh + 1) * 4, :, mt, :].rearrange("m p f -> p m f")
                mldst = scr["ml"].ap().rearrange("mo p (kc c) -> mo p kc c", c=128)[
                    nh * 4:(nh + 1) * 4, :, mt, :].rearrange("m p f -> p m f")
                dsp(mdst, y3a[:, 0:512].rearrange("p (m f) -> p m f", m=4),
                    ["mstg"], ["scrm"])
                dsp(mldst, y3a[:, 512:1024].rearrange("p (m f) -> p m f", m=4),
                    ["mstg"], ["scrm"])

        def ln_stats(src_ap, src_rg, need_mean):
            if need_mean:
                cpy("dve", sq16, src_ap, [src_rg], ["sq16"])
                for mt in range(DTn):
                    mm(ps_s[0:1, 0:512], onescol16[:], sq16[:, mt * TC:(mt + 1) * TC],
                       ["sq16", "onescol"], ["ps_s"], start=(mt == 0), stop=(mt == DTn - 1))
            act(sq16, src_ap, AF.Square, [src_rg], ["sq16", "dpx"])
            for mt in range(DTn):
                mm(ps_b[0:1, 0:512], onescol16[:], sq16[:, mt * TC:(mt + 1) * TC],
                   ["sq16", "onescol"], ["ps_b"], start=(mt == 0), stop=(mt == DTn - 1))

        def ln_rows(rms_only):
            if rms_only:
                tsc("dve", tmpr[:], ps_b[0:1, 0:512], 1.0 / D, None, ALU.mult, ALU.bypass,
                    ["ps_b"], ["tmpr"])
            else:
                tsc("dve", mr[:], ps_s[0:1, 0:512], 1.0 / D, None, ALU.mult, ALU.bypass,
                    ["ps_s"], ["mr"])
                tsc("dve", tmpr[:], ps_b[0:1, 0:512], 1.0 / D, None, ALU.mult, ALU.bypass,
                    ["ps_b"], ["tmpr"])
                tt("dve", rstdr[:], mr[:], mr[:], ALU.mult, ["mr"], ["rstdr"])
                tt("dve", tmpr[:], tmpr[:], rstdr[:], ALU.subtract, ["tmpr", "rstdr"], ["tmpr"])
            act(tmpr[:], tmpr[:], AF.Ln, ["tmpr"], ["tmpr"], bias=EPS)
            act(rstdr[:], tmpr[:], AF.Exp, ["tmpr"], ["rstdr"], scale=-0.5)

        def hilo_row(dst16, dst_rg, src_row, src_rg):
            cpy("dve", dst16[:, 0:TC], src_row, [src_rg], [dst_rg])
            tt("dve", dst16[:, TC:2 * TC], src_row, dst16[:, 0:TC], ALU.subtract,
               [src_rg, dst_rg], [dst_rg])

        def bcast2(dst_ps, ps_rg, row16, row_rg):
            mm(dst_ps, onesrow16[0:1, :], row16[:, 0:TC], [row_rg, "onesrow"], [ps_rg],
               start=True, stop=False)
            mm(dst_ps, onesrow16[0:1, :], row16[:, TC:2 * TC], [row_rg, "onesrow"], [ps_rg],
               start=False, stop=True)

        def ln_apply_plain(src_ap, src_rg, dst_f32_ap, dst_rg, add_to=None, add_rg=None,
                           dst16_ap=None, dst16_rg=None):
            tt("dve", tmpr[:], mr[:], rstdr[:], ALU.mult, ["mr", "rstdr"], ["tmpr"])
            hilo_row(rs16, "rs16", rstdr[:], "rstdr")
            hilo_row(mm16, "mm16", tmpr[:], "tmpr")
            bcast2(ps_s[:, 0:512], "ps_s", rs16, "rs16")
            bcast2(ps_b[:, 0:512], "ps_b", mm16, "mm16")
            for mt in range(DTn):
                sl = slice(mt * TC, (mt + 1) * TC)
                if add_to is not None:
                    tt("dve", t1f[:], src_ap[:, sl], ps_s[:, 0:512], ALU.mult,
                       [src_rg, "ps_s"], ["t1f"])
                    tt("dve", t1f[:], t1f[:], ps_b[:, 0:512], ALU.subtract,
                       ["t1f", "ps_b"], ["t1f"])
                    tt("dve", add_to[:, sl], add_to[:, sl], t1f[:], ALU.add,
                       [add_rg, "t1f"], [add_rg])
                elif dst16_ap is not None:
                    tt("dve", t1f[:], src_ap[:, sl], ps_s[:, 0:512], ALU.mult,
                       [src_rg, "ps_s"], ["t1f"])
                    tt("dve", dst16_ap[:, sl], t1f[:], ps_b[:, 0:512], ALU.subtract,
                       ["t1f", "ps_b"], [dst16_rg])
                else:
                    tt("dve", dst_f32_ap[:, sl], src_ap[:, sl], ps_s[:, 0:512], ALU.mult,
                       [src_rg, "ps_s"], [dst_rg])
                    tt("dve", dst_f32_ap[:, sl], dst_f32_ap[:, sl], ps_b[:, 0:512],
                       ALU.subtract, [dst_rg, "ps_b"], [dst_rg])

        def emit_mamba(k, i):
            if phase < 2:
                return
            ln_stats(rres[:], "rres", False)
            ln_rows(True)
            hilo_row(rs16, "rs16", rstdr[:], "rstdr")
            bcast2(ps_s[:, 0:512], "ps_s", rs16, "rs16")
            for mt in range(DTn):
                sl = slice(mt * TC, (mt + 1) * TC)
                tt("dve", t1f[:], rres[:, sl], ps_s[:, 0:512], ALU.mult,
                   ["rres", "ps_s"], ["t1f"])
                act(xn16[:, sl], t1f[:], AF.Copy, ["t1f", f"normw{i}"], ["xn16"],
                    scale=normw[i][:, mt:mt + 1])

            for eb in range(EBn):
                slp = eb % 2
                wsl = wA[:, slp * DTn * 128:(slp + 1) * DTn * 128]
                wrg = f"wA{slp}"
                dsp(wsl, scr[f"L{i}_wxb"][eb], [f"swxb{i}"], [wrg])
                for kc in range(DTn):
                    mm(ps_a[:, 0:512], wsl[:, kc * 128:(kc + 1) * 128],
                       xn16[:, kc * TC:(kc + 1) * TC], [wrg, "xn16"], ["ps_a"],
                       start=(kc == 0), stop=(kc == DTn - 1))
                xzs = xzr[:, slp * (TC + 3):(slp + 1) * (TC + 3)]
                xrg = f"xzr{slp}"
                act(xzs[:, 3:TC + 3], ps_a[:, 0:512], AF.Copy, ["ps_a"], [xrg])
                cpy("pool", xzs[:, 0:3], xztail[i][:, eb * 3:eb * 3 + 3],
                    [f"xztail{i}"], [xrg])
                cpy("pool", xztail[i][:, eb * 3:eb * 3 + 3], xzs[:, TC:TC + 3],
                    [xrg], [f"xztail{i}"])
                dacc = delta[:, slp * TC:(slp + 1) * TC]
                drg = f"delta{slp}"
                tsc("dve", dacc, xzs[:, 0:TC], cwsb[i][:, eb * DCONV:eb * DCONV + 1],
                    None, ALU.mult, ALU.bypass, [xrg, f"cw{i}"], [drg])
                for kk in range(1, DCONV):
                    stt("dve", dacc, xzs[:, kk:kk + TC],
                        cwsb[i][:, eb * DCONV + kk:eb * DCONV + kk + 1], dacc,
                        ALU.mult, ALU.add, [xrg, f"cw{i}", drg], [drg])
                act(xb16[:, eb * TC:(eb + 1) * TC], dacc, AF.Silu,
                    [drg, f"convb{i}"], ["xb16"], bias=convb[i][:, eb:eb + 1])

            if phase < 3:
                return
            for kc in range(EBn):
                mm(ps_a[0:RNK, 0:512], xpd16[i][:, kc * RNK:(kc + 1) * RNK],
                   xb16[:, kc * TC:(kc + 1) * TC], [f"xpd{i}", "xb16"], ["ps_a"],
                   start=(kc == 0), stop=(kc == EBn - 1))
                mm(ps_b[0:NST, 0:512], xpB16[i][:, kc * NST:(kc + 1) * NST],
                   xb16[:, kc * TC:(kc + 1) * TC], [f"xpB{i}", "xb16"], ["ps_b"],
                   start=(kc == 0), stop=(kc == EBn - 1))
                mm(ps_s[0:NST, 0:512], xpC16[i][:, kc * NST:(kc + 1) * NST],
                   xb16[:, kc * TC:(kc + 1) * TC], [f"xpC{i}", "xb16"], ["ps_s"],
                   start=(kc == 0), stop=(kc == EBn - 1))
            act(dbcd16[RNK * i:RNK * (i + 1), :], ps_a[0:RNK, 0:512], AF.Copy, ["ps_a"], ["dbcd"])
            cpy("pool", b16[:, 0:ADV], btail[i][:], [f"btail{i}"], ["b16"])
            act(b16[:, ADV:ADV + TC], ps_b[0:NST, 0:512], AF.Copy, ["ps_b"], ["b16"])
            cpy("pool", btail[i][:], b16[:, TC:TC + ADV], ["b16"], [f"btail{i}"])
            act(c16[:], ps_s[0:NST, 0:512], AF.Copy, ["ps_s"], ["c16"])

            for a in range(NADV):
                transp(ps_t[:, a * NST:(a + 1) * NST], b16[0:NST, a * ADV: a * ADV + 128],
                       ident16[0:NST, 0:NST], ["b16", "ident"], ["ps_t0"])
            cpy("dve", btw16[:], ps_t[:, 0:NADV * NST], ["ps_t0"], ["btw"])
            for a in range(NADV):
                o = wbeta[:, a * NST * ADV:(a + 1) * NST * ADV].rearrange(
                    "p (n j) -> p n j", n=NST)
                i0 = betap16[:].unsqueeze(1).broadcast_to([128, NST, ADV])
                i1 = btw16[:, a * NST:(a + 1) * NST].unsqueeze(2).broadcast_to([128, NST, ADV])
                tt("dve", o, i0, i1, ALU.mult, ["btw", "betap"], [f"wbeta{a}"])

            for n in range(NST):
                slc = n % 2
                prg = f"ps_v{slc}"
                mm(ps_v[:, slc * 1024: slc * 1024 + 512],
                   sel16[:, n * 128:(n + 1) * 128], c16[:], ["sel", "c16"], [prg])
                cpy("dve", crep[:, n * TC:(n + 1) * TC],
                    ps_v[:, slc * 1024: slc * 1024 + 512], [prg], ["crep"])

            if phase < 4:
                return
            hf = h16[:].bitcast(F32)
            HLO = [f"h{n2}" for n2 in range(8)]
            HHI = [f"h{n2}" for n2 in range(8, NST)]

            def dp_y3(eb):
                dpx = big16[:, 6144:6656]
                tsc("dve", dpx, xb16[:, eb * TC:(eb + 1) * TC], dpv[i][:, eb:eb + 1],
                    None, ALU.mult, ALU.bypass, ["xb16", f"dp{i}"], ["dpx"])
                tt("pool", t1f[:], t1f[:], dpx, ALU.add, ["t1f", "dpx"], ["t1f"])

            def y3_emit(eb):
                szs = sz16[:, (eb % 3) * TC:(eb % 3 + 1) * TC]
                tt("pool", y3a[:, eb * TC:(eb + 1) * TC], t1f[:], szs, ALU.mult,
                   ["t1f", f"sz{eb % 3}"], ["y3a"])

            def z_jit(eb):
                slp = eb % 2
                wsl2 = wB[:, slp * EBn * 128: slp * EBn * 128 + DTn * 128]
                zrg = f"wB{slp}"
                dsp(wsl2, scr[f"L{i}_wz"][eb], [f"swz{i}"], [zrg])
                for kc in range(DTn):
                    mm(ps_b[:, 0:512], wsl2[:, kc * 128:(kc + 1) * 128],
                       xn16[:, kc * TC:(kc + 1) * TC], [zrg, "xn16"], ["ps_b"],
                       start=(kc == 0), stop=(kc == DTn - 1))
                act(sz16[:, (eb % 3) * TC:(eb % 3 + 1) * TC], ps_b[:, 0:512], AF.Silu,
                    ["ps_b"], [f"sz{eb % 3}"])

            def utw_cpy(eb):
                slp = eb % 2
                cpy("act", utw16[:, slp * 1024:(slp + 1) * 1024], ps_t[:, 0:NADV * 128],
                    ["ps_t0"], [f"utw{slp}"])

            def stage_a(eb):
                slp = eb % 2
                mm(ps_a[:, 0:512],
                   dtw16p[RNK * i:RNK * (i + 1), eb * 128:(eb + 1) * 128],
                   dbcd16[RNK * i:RNK * (i + 1), :], [f"dtw{i}", "dbcd"], ["ps_a"])
                dl = delta[:, slp * TC:(slp + 1) * TC]
                drg = f"delta{slp}"
                sp = spl[:]
                srg = "spl0"
                act(dl, ps_a[:, 0:512], AF.Relu, ["ps_a", f"dtb{i}"], [drg],
                    bias=dtb[i][:, eb:eb + 1])
                act(sp, ps_a[:, 0:512], AF.Abs, ["ps_a", f"dtb{i}"], [srg],
                    bias=dtb[i][:, eb:eb + 1])
                act(sp, sp, AF.Exp, [srg], [srg], scale=-1.0)
                act(sp, sp, AF.Ln, [srg], [srg], bias=1.0)
                tt("pool", dl, dl, sp, ALU.add, [drg, srg], [drg])
                u16s = u16[:, slp * (TC + ADV):(slp + 1) * (TC + ADV)]
                u16rg = f"u16{slp}"
                cpy("pool", u16s[:, 0:ADV],
                    utail[i][:, eb * ADV:(eb + 1) * ADV], [f"utail{i}"], [u16rg])
                tt("pool", u16s[:, ADV:TC + ADV],
                   dl, xb16[:, eb * TC:(eb + 1) * TC], ALU.mult, [drg, "xb16"], [u16rg])
                cpy("pool", utail[i][:, eb * ADV:(eb + 1) * ADV],
                    u16s[:, TC:TC + ADV], [u16rg], [f"utail{i}"])
                for a in range(NADV):
                    transp(ps_t[:, a * 128:(a + 1) * 128],
                           u16s[:, a * ADV: a * ADV + 128],
                           ident16[:],
                           [u16rg, "ident"], ["ps_t0"])

            def stage_b(eb):
                slp = eb % 2
                utws = utw16[:, slp * 1024:(slp + 1) * 1024]
                urg = f"utw{slp}"
                dl = delta[:, slp * TC:(slp + 1) * TC]
                drg = f"delta{slp}"
                for g in range(NST // 2):
                    gsl = g % 2
                    vg = ps_v[:, gsl * 1024:(gsl + 1) * 1024]
                    vrg = f"ps_v{gsl}"
                    for a in range(NADV):
                        for nl in range(2):
                            n = 2 * g + nl
                            mm(vg[:, nl * 512 + a * ADV: nl * 512 + (a + 1) * ADV],
                               utws[:, a * 128:(a + 1) * 128],
                               wbeta[:, a * NST * ADV + n * ADV: a * NST * ADV + (n + 1) * ADV],
                               [urg, f"wbeta{a}"], [vrg])
                    for nl in range(2):
                        n = 2 * g + nl
                        afs = a_f[:, (n % 4) * TC:(n % 4 + 1) * TC]
                        arg = f"a_f{n % 4}"
                        act(afs, dl, AF.Exp, [drg, f"wneg{i}"], [arg],
                            scale=wneg[i][:, eb * NST + n: eb * NST + n + 1])
                        scan(h16[:, n * TC:(n + 1) * TC], afs, vg[:, nl * 512:(nl + 1) * 512],
                             carry[i][:, eb * NST + n: eb * NST + n + 1],
                             [arg, vrg, f"carry{i}"],
                             [f"h{n}"] + (["hf32"] if n >= 8 else []))
                    if g == 3:
                        cpy("dve", carry[i][:, eb * NST:eb * NST + 8],
                            h16[:, 0:8 * TC].rearrange("p (n t) -> p n t", n=8)[:, :, TC - 1],
                            HLO, [f"carry{i}"])
                        tt("dve", h16[:, 0:8 * TC], h16[:, 0:8 * TC], crep[:, 0:8 * TC],
                           ALU.mult, HLO + ["crep", f"carry{i}"], HLO)
                    if g == 5:
                        cpy("dve", carry[i][:, eb * NST + 8:eb * NST + 12],
                            h16[:, 8 * TC:12 * TC].rearrange("p (n t) -> p n t", n=4)[:, :, TC - 1],
                            HHI[0:4], [f"carry{i}"])
                        tt("pool", sq16[:, 0:4 * TC], h16[:, 8 * TC:12 * TC],
                           crep[:, 8 * TC:12 * TC],
                           ALU.mult, HHI[0:4] + ["crep", f"carry{i}"], ["sq16"])
                        if eb > 0:
                            dp_y3(eb - 1)
                            y3_emit(eb - 1)
                cpy("dve", carry[i][:, eb * NST + 12:(eb + 1) * NST],
                    h16[:, 12 * TC:NST * TC].rearrange("p (n t) -> p n t", n=4)[:, :, TC - 1],
                    HHI[4:8], [f"carry{i}"])
                tt("dve", h16[:, 12 * TC:NST * TC], h16[:, 12 * TC:NST * TC],
                   crep[:, 12 * TC:NST * TC], ALU.mult,
                   HHI[4:8] + ["crep", f"carry{i}"], HHI[4:8])
                tt("dve", h16[:, 0:4 * TC], h16[:, 0:4 * TC], sq16[:, 0:4 * TC],
                   ALU.add, HLO + ["sq16"], HLO[0:4])
                tt("dve", h16[:, 4 * TC:8 * TC], h16[:, 4 * TC:8 * TC],
                   h16[:, 12 * TC:NST * TC],
                   ALU.add, HLO[4:8] + HHI[4:8], HLO[4:8])
                if eb + 1 < EBn:
                    utw_cpy(eb + 1)
                    z_jit(eb + 1)
                tt("dve", hf[:, 2048:4096], h16[:, 0:4 * TC], h16[:, 4 * TC:8 * TC],
                   ALU.add, HLO, ["hf32"] + HHI)
                tt("dve", hf[:, 2048:3072], hf[:, 2048:3072], hf[:, 3072:4096],
                   ALU.add, ["hf32"], ["hf32"])
                tt("dve", t1f[:], hf[:, 2048:2560], hf[:, 2560:3072],
                   ALU.add, ["hf32"], ["t1f"])

            for ebi in range(EBn + 1):
                if ebi < EBn:
                    stage_a(ebi)
                    if ebi == 0:
                        utw_cpy(0)
                        z_jit(0)
                if ebi >= 1:
                    stage_b(ebi - 1)
            dp_y3(EBn - 1)
            y3_emit(EBn - 1)

            if phase < 5:
                return
            for mt in range(DTn):
                slp = mt % 2
                wsl3 = wB[:, slp * EBn * 128:(slp + 1) * EBn * 128]
                org = f"wB{slp}"
                dsp(wsl3, scr[f"L{i}_wo"][mt], [f"swo{i}"], [org])
                pps = ps_a if slp == 0 else ps_b
                psrg = "ps_a" if slp == 0 else "ps_b"
                for kc in range(EBn):
                    mm(pps[:, 0:512], wsl3[:, kc * 128:(kc + 1) * 128],
                       y3a[:, kc * TC:(kc + 1) * TC], [org, "y3a"], [psrg],
                       start=(kc == 0), stop=(kc == EBn - 1))
                tt("dve", rres[:, mt * TC:(mt + 1) * TC], rres[:, mt * TC:(mt + 1) * TC],
                   pps[:, 0:512], ALU.add, ["rres", psrg], ["rres"])

        def emit_chunk(k):
            dsp(x0[:].rearrange("p (m t) -> p m t", m=DTn),
                xT_d[:, k * TC:(k + 1) * TC].rearrange("(m p) t -> p m t", p=128),
                [], ["x0"])
            ln_stats(x0[:], "x0", True)
            ln_rows(False)
            ln_apply_plain(x0[:], "x0", rres[:], "rres")
            for i in range(2):
                emit_mamba(k, i)
            if phase < 6:
                dsp(yT_d[:, k * TC:(k + 1) * TC].rearrange("(m p) t -> p m t", p=128),
                    x0[:].rearrange("p (m t) -> p m t", m=DTn), ["x0"], ["yout"])
                return
            ln_stats(rres[:], "rres", True)
            ln_rows(False)
            ln_apply_plain(rres[:], "rres", None, None, add_to=x0[:], add_rg="x0")
            ln_stats(x0[:], "x0", True)
            ln_rows(False)
            xlo_tiles = [
                (delta[:].bitcast(BF16)[:, 0:512], "delta0"),
                (delta[:].bitcast(BF16)[:, 1024:1536], "delta1"),
                (a_f[:].bitcast(BF16)[:, 0:512], "a_f0"),
                (a_f[:].bitcast(BF16)[:, 1024:1536], "a_f1"),
                (a_f[:].bitcast(BF16)[:, 2048:2560], "a_f2"),
                (a_f[:].bitcast(BF16)[:, 3072:3584], "a_f3"),
                (spl[:].bitcast(BF16)[:, 0:512], "spl0"),
                (sz16[:, 0:512], "sz0"),
            ]
            tt("dve", tmpr[:], mr[:], rstdr[:], ALU.mult, ["mr", "rstdr"], ["tmpr"])
            hilo_row(rs16, "rs16", rstdr[:], "rstdr")
            hilo_row(mm16, "mm16", tmpr[:], "tmpr")
            bcast2(ps_s[:, 0:512], "ps_s", rs16, "rs16")
            bcast2(ps_b[:, 0:512], "ps_b", mm16, "mm16")
            for mt in range(DTn):
                sl = slice(mt * TC, (mt + 1) * TC)
                tt("dve", t1f[:], x0[:, sl], ps_s[:, 0:512], ALU.mult,
                   ["x0", "ps_s"], ["t1f"])
                tt("dve", t1f[:], t1f[:], ps_b[:, 0:512], ALU.subtract,
                   ["t1f", "ps_b"], ["t1f"])
                cpy("dve", xn16[:, sl], t1f[:], ["t1f"], ["xn16"])
                lo_ap, lo_rg = xlo_tiles[mt]
                tt("dve", lo_ap, t1f[:], xn16[:, sl], ALU.subtract,
                   ["t1f", "xn16"], [lo_rg])
            ffns = wbeta[:].bitcast(F32)
            pend = []
            for a in range(NADV):
                t = P.wmap.get(f"wbeta{a}")
                if t:
                    pend.append(t)
                pend.extend(P.rmap.get(f"wbeta{a}", ()))
            P.rmap.setdefault("ffns", []).extend(pend)
            for mt in range(DTn):
                slw = mt % 2
                wsl = wA[:, slw * DTn * 128:(slw + 1) * DTn * 128]
                wrg = f"wA{slw}"
                wsl2b = wB[:, slw * EBn * 128: slw * EBn * 128 + DTn * 128]
                wrg2 = f"wB{slw}"
                dsp(wsl, scr["m"][mt], ["scrm"], [wrg])
                dsp(wsl2b, scr["ml"][mt], ["scrm"], [wrg2])
                pps = ps_a if mt % 2 == 0 else ps_b
                psrg = "ps_a" if mt % 2 == 0 else "ps_b"
                for kc in range(DTn):
                    lo_ap, lo_rg = xlo_tiles[kc]
                    mm(pps[:, 0:512], wsl[:, kc * 128:(kc + 1) * 128],
                       xn16[:, kc * TC:(kc + 1) * TC], [wrg, "xn16"], [psrg],
                       start=(kc == 0), stop=False)
                    mm(pps[:, 0:512], wsl[:, kc * 128:(kc + 1) * 128],
                       lo_ap, [wrg, lo_rg], [psrg], start=False, stop=False)
                    mm(pps[:, 0:512], wsl2b[:, kc * 128:(kc + 1) * 128],
                       xn16[:, kc * TC:(kc + 1) * TC], [wrg2, "xn16"], [psrg],
                       start=False, stop=(kc == DTn - 1))
                tt("dve", ffns[:, mt * TC:(mt + 1) * TC], pps[:, 0:512],
                   fbias[:, mt:mt + 1].broadcast_to([128, TC]), ALU.add,
                   [psrg, "fbias"], ["ffns"])
            ln_stats(ffns[:, 0:DTn * TC], "ffns", True)
            ln_rows(False)
            ln_apply_plain(ffns[:, 0:DTn * TC], "ffns", None, None, add_to=x0[:], add_rg="x0")
            for a in range(NADV):
                if "ffns" in P.wmap:
                    P.wmap[f"wbeta{a}"] = P.wmap["ffns"]
                P.rmap[f"wbeta{a}"] = list(P.rmap.get("ffns", ()))
            dsp(yT_d[:, k * TC:(k + 1) * TC].rearrange("(m p) t -> p m t", p=128),
                x0[:].rearrange("p (m t) -> p m t", m=DTn), ["x0"], ["yout"])

        def xfer(dst, srcrg):
            if srcrg in P.wmap:
                P.wmap[dst] = P.wmap[srcrg]
            P.rmap[dst] = list(P.rmap.get(srcrg, ()))
        xfer("xb16", "trh")
        xfer("xn16", "trl")
        xfer("sq16", "trl")
        xfer("dpx", "trl")
        xfer("crep", "tih")
        for a in range(NADV):
            xfer(f"wbeta{a}", "til")
        for n2 in range(NST):
            xfer(f"h{n2}", "h16")
        xfer("y3a", "mstg")
        P.rmap.setdefault("y3a", []).extend(P.rmap.get("y3aW", ()))
        for s2 in ("a", "b", "c", "d"):
            for dstrg in ("delta0", "delta1"):
                xfer(dstrg, "cs0" + s2)
            for dstrg in ("a_f0", "a_f1"):
                xfer(dstrg, "cs1" + s2)
        for k in range(NCH):
            emit_chunk(k)

        if dbg:
            dsp(mdbg_d, scr["m"][:, :], ["scrm"], ["mdbg"])
            P.barrier("sp", ["mdbg"])
        P.barrier("sp", ["yout"])

        def mk_emitter(engname):
            def emitter(eng):
                hw = {s: 0 for s in sems}
                for entry in P.ops[engname]:
                    typ, waits, fn, slot, seq = entry[0], entry[1], entry[2], entry[3], entry[4]
                    qn = entry[5] if len(entry) > 5 else None
                    for (src, val) in waits:
                        v = val * (16 if "D" in src else 1)
                        if src == engname == "pe":
                            continue
                        if v > hw[src]:
                            eng.wait_ge(sems[src], v)
                            hw[src] = v
                    if typ == "wait":
                        continue
                    if typ == "op":
                        inst = fn(eng)
                        inst.then_inc(sems[engname], 1)
                    else:
                        sn = f"{qn}D{slot}"
                        if seq > 1:
                            v = 16 * (seq - 1)
                            if v > hw[sn]:
                                eng.wait_ge(sems[sn], v)
                                hw[sn] = v
                        inst = fn(eng)
                        inst.then_inc(sems[sn], 16)
            return emitter

        block.tensor(mk_emitter("pe"))
        block.vector(mk_emitter("dve"))
        block.scalar(mk_emitter("act"))
        block.gpsimd(mk_emitter("pool"))
        block.sync(mk_emitter("sp"))

    return nc


_NC_CACHE = {}


def prep_common(inputs):
    def f32c(a):
        return np.ascontiguousarray(np.asarray(a, dtype=np.float32))

    common = {}
    for i in range(2):
        p = f"L{i}_"
        inw = f32c(inputs["L0_attn_in_proj_w"][i])
        common[p + "inw_xbT"] = f32c(inw[:E].T)
        common[p + "inw_zT"] = f32c(inw[E:].T)
        common[p + "convw"] = f32c(np.transpose(
            inputs["L0_attn_conv_w"][i].reshape(EBn, 128, DCONV), (1, 0, 2)).reshape(128, EBn * DCONV))
        common[p + "convb"] = f32c(np.transpose(
            np.asarray(inputs["L0_attn_conv_b"][i]).reshape(EBn, 128)))
        xp = f32c(inputs["L0_attn_x_proj_w"][i])
        def tile_e_rows(arr):
            r = arr.shape[1]
            return f32c(np.transpose(arr.reshape(EBn, 128, r), (1, 0, 2)).reshape(128, EBn * r))
        common[p + "xpdT"] = tile_e_rows(np.asarray(xp[0:RNK].T))
        common[p + "xpBT"] = tile_e_rows(np.asarray(xp[RNK:RNK + NST].T))
        common[p + "xpCT"] = tile_e_rows(np.asarray(xp[RNK + NST:RNK + 2 * NST].T))
        common[p + "dtwT"] = f32c(inputs["L0_attn_dt_w"][i].T)
        common[p + "dtb"] = f32c(np.transpose(
            np.asarray(inputs["L0_attn_dt_b"][i]).reshape(EBn, 128)))
        common[p + "alog"] = tile_e_rows(np.asarray(inputs["L0_attn_A_log"][i]))
        common[p + "dp"] = f32c(np.transpose(
            np.asarray(inputs["L0_attn_Dp"][i]).reshape(EBn, 128)))
        common[p + "owT"] = f32c(inputs["L0_attn_out_w"][i].T)
        common[p + "normw"] = f32c(np.transpose(
            np.asarray(inputs["L0_attn_norm_w"][i]).reshape(DTn, 128)))
    common["Wr"] = f32c(inputs["L0_ffn_w_freq_real"])
    common["Wi"] = f32c(inputs["L0_ffn_w_freq_imag"])
    common["fbias"] = f32c(np.transpose(
        np.asarray(inputs["L0_ffn_bias"]).reshape(DTn, 128)))

    common["ident16"] = np.eye(128, dtype=BF)
    jj = np.arange(ADV)[None, :]
    rr = np.arange(128)[:, None]
    bp = np.where(rr <= 64 + jj, BETA ** ((64 + jj) - rr), 0.0).astype(np.float32)
    common["betap16"] = bp
    common["onesrow16"] = np.ones((1, 128), dtype=BF)
    common["constrow"] = np.tile(np.array([[EPS, 1.0, -1.0, -0.5, 1.0 / D, 0.0, 2.0, 0.5]],
                                           dtype=np.float32), (128, 1))
    common["onescol16"] = np.ones((128, 1), dtype=BF)
    sel = np.zeros((NST, NST * 128), dtype=np.float32)
    for n in range(NST):
        sel[n, n * 128:(n + 1) * 128] = 1.0
    common["sel16"] = sel.astype(BF)
    grid = np.arange(D)
    ang = 2.0 * np.pi * np.outer(grid, grid) / D
    cf = np.cos(ang).astype(np.float32)
    sf = np.sin(ang).astype(np.float32)
    def cstile(a):
        return np.ascontiguousarray(
            np.transpose(a.reshape(DTn, 128, 2, 512), (0, 2, 1, 3)).reshape(DTn * 2, 128, 512))
    for nm, arr in (("Cf", cf), ("Sf", sf), ("Sfn", -sf)):
        hi = arr.astype(BF)
        lo = (arr - hi.astype(np.float32)).astype(BF)
        common[nm + "_hi"] = cstile(hi)
        common[nm + "_lo"] = cstile(lo)
    return common


def kernel(**inputs):
    x = np.asarray(inputs["L0_attn_x"], dtype=np.float32)

    def f32c(a):
        return np.ascontiguousarray(np.asarray(a, dtype=np.float32))

    common = prep_common(inputs)

    if "nc" not in _NC_CACHE:
        _NC_CACHE["nc"] = build_nc()
    nc = _NC_CACHE["nc"]

    in_maps = []
    for c in range(NCORES):
        b = c // 2
        m = dict(common)
        m["xT"] = f32c(x[b].T)
        in_maps.append(m)

    res = run_bass_kernel_spmd(nc, in_maps, core_ids=list(range(NCORES)))
    out = np.empty((B, L, D), dtype=np.float32)
    for b in range(B):
        out[b] = np.asarray(res.results[2 * b]["yT"]).T
    return out
